# version 15
# baseline (speedup 1.0000x reference)
import numpy as np
import concourse.bass as bass
import concourse.mybir as mybir
from concourse.bass_utils import run_bass_kernel_spmd
from contextlib import ExitStack

F32 = mybir.dt.float32
BF16 = mybir.dt.bfloat16
AF = mybir.ActivationFunctionType
ALU = mybir.AluOpType
AXX = mybir.AxisListType.X

D = 1024
S = 4096
T = 2048
LAT = 256
NH = 16
HD = 64
DH = 4096
EPS = 1e-5

ENGINES = ("tensor", "vector", "scalar", "sync", "gpsimd")


class Prog:
    def __init__(self, nc):
        self.nc = nc
        self.ops = {k: [] for k in ENGINES}
        self.sem = {}
        self.cnt = {}
        self.free = {}

    def add_sem(self, stack, name):
        self.sem[name] = stack.enter_context(self.nc.semaphore(name))
        self.cnt[name] = 0

    def do(self, eng, fn, inc=None, n=1):
        if inc is None:
            self.ops[eng].append(fn)
            return None
        self.cnt[inc] += n
        s = self.sem[inc]
        self.ops[eng].append(lambda e, f=fn, s=s, n=n: f(e).then_inc(s, n))
        return self.cnt[inc]

    def wait(self, eng, sem, val):
        assert val <= self.cnt[sem], (sem, val, self.cnt[sem])
        if val <= 0:
            return
        s = self.sem[sem]
        self.ops[eng].append(lambda e, s=s, v=val: e.wait_ge(s, v))

    def wfull(self, eng, sem):
        self.wait(eng, sem, self.cnt[sem])

    def wait_free(self, eng, key):
        if key in self.free:
            sname, v = self.free[key]
            self.wait(eng, sname, v)

    def set_free(self, key, sname, v):
        self.free[key] = (sname, v)

    def barrier(self):
        tgt = {k: self.cnt[k] for k in self.sem}
        for eng in ENGINES:
            for sname, v in tgt.items():
                self.wait(eng, sname, v)

    def emit_block(self):
        nc = self.nc
        ops = self.ops
        with nc.Block() as blk:
            @blk.tensor
            def _(e):
                for op in ops["tensor"]:
                    op(e)

            @blk.vector
            def _(e):
                for op in ops["vector"]:
                    op(e)

            @blk.scalar
            def _(e):
                for op in ops["scalar"]:
                    op(e)

            @blk.sync
            def _(e):
                for op in ops["sync"]:
                    op(e)

            @blk.gpsimd
            def _(e):
                for op in ops["gpsimd"]:
                    op(e)
        self.ops = {k: [] for k in ENGINES}


def build_nc():
    nc = bass.Bass("TRN2", target_bir_lowering=False)

    xf = nc.dram_tensor("xf", [S, D], F32, kind="ExternalInput").ap()
    wq_d = nc.dram_tensor("wq", [D, D], F32, kind="ExternalInput").ap()
    wkb_d = nc.dram_tensor("wkb", [D, LAT], F32, kind="ExternalInput").ap()
    wkaT_d = nc.dram_tensor("wkaT", [D, LAT], F32, kind="ExternalInput").ap()
    wvaT_d = nc.dram_tensor("wvaT", [D, LAT], F32, kind="ExternalInput").ap()
    wvbT_d = nc.dram_tensor("wvbT", [LAT, D], F32, kind="ExternalInput").ap()
    woT_d = nc.dram_tensor("woT", [D, D], F32, kind="ExternalInput").ap()
    w1b_d = nc.dram_tensor("w1b", [16, 128, 8, 256], F32, kind="ExternalInput").ap()
    w2b_d = nc.dram_tensor("w2b", [16, 128, 8, 256], F32, kind="ExternalInput").ap()
    h1b_d = nc.dram_tensor("h1b", [32, 128, 32, 128], F32, kind="ExternalInput").ap()
    h2b_d = nc.dram_tensor("h2b", [32, 128, 32, 128], F32, kind="ExternalInput").ap()
    w3T_d = nc.dram_tensor("w3T", [DH, D], F32, kind="ExternalInput").ap()
    s1_d = nc.dram_tensor("s1", [128, 32], F32, kind="ExternalInput").ap()
    s2_d = nc.dram_tensor("s2", [128, 32], F32, kind="ExternalInput").ap()
    sh1_d = nc.dram_tensor("sh1", [128, 32], F32, kind="ExternalInput").ap()
    sh2_d = nc.dram_tensor("sh2", [128, 32], F32, kind="ExternalInput").ap()
    s3b_d = nc.dram_tensor("s3b", [128, D], F32, kind="ExternalInput").ap()
    ident_d = nc.dram_tensor("ident", [128, 128], F32, kind="ExternalInput").ap()
    rtc_d = nc.dram_tensor("rtc", [128, 8 * 128], F32, kind="ExternalInput").ap()
    ones128_d = nc.dram_tensor("ones128", [128, 64], F32, kind="ExternalInput").ap()
    onesc_d = nc.dram_tensor("onesc", [128, 1], F32, kind="ExternalInput").ap()
    eh_d = nc.dram_tensor("eh", [128, 128], F32, kind="ExternalInput").ap()
    out_d = nc.dram_tensor("out", [T, D], F32, kind="ExternalOutput").ap()
    x1sp_d = nc.dram_tensor("x1sp", [T, D], F32, kind="Internal").ap()
    w3st_d = nc.dram_tensor("w3st", [DH, D], BF16, kind="Internal").ap()

    with ExitStack() as outer:
        oe = outer.enter_context
        p = Prog(nc)
        for sname in ("dg", "ds", "t", "v", "s", "dl0", "dl1", "dl2",
                      "dst0", "dst1",
                      "gq0", "gq1", "ga0", "ga1", "gn0", "gn1", "gw0", "gw1",
                      "gh0", "gh1"):
            p.add_sem(outer, sname)

        ident = oe(nc.sbuf_tensor("ident_s", [128, 128], F32))
        rtc = oe(nc.sbuf_tensor("rtc_s", [128, 8 * 128], BF16))
        ones128 = oe(nc.sbuf_tensor("ones128_s", [128, 64], BF16))
        onesc = oe(nc.sbuf_tensor("onesc_s", [128, 1], BF16))
        s1s = oe(nc.sbuf_tensor("s1_s", [128, 32], F32))
        s2s = oe(nc.sbuf_tensor("s2_s", [128, 32], F32))
        sh1s = oe(nc.sbuf_tensor("sh1_s", [128, 32], F32))
        sh2s = oe(nc.sbuf_tensor("sh2_s", [128, 32], F32))
        s3bs = oe(nc.sbuf_tensor("s3b_s", [128, D], F32))
        stats = oe(nc.sbuf_tensor("stats", [128, 64], F32))
        ps = [oe(nc.psum_tensor(f"ps{i}", [128, 1024], F32)) for i in range(4)]

        DO, W, WF, SF = p.do, p.wait, p.wait_free, p.set_free

        DO("sync", lambda e: e.dma_start(ident[:], ident_d), inc="ds", n=16)
        DO("gpsimd", lambda e: e.dma_start(rtc[:], rtc_d), inc="dg", n=16)
        DO("gpsimd", lambda e: e.dma_start(ones128[:], ones128_d), inc="dg", n=16)
        DO("gpsimd", lambda e: e.dma_start(onesc[:], onesc_d), inc="dg", n=16)
        for dst, src in ((s1s, s1_d), (s2s, s2_d), (sh1s, sh1_d), (sh2s, sh2_d),
                         (s3bs, s3b_d)):
            DO("sync", lambda e, d=dst, s_=src: e.dma_start(d[:], s_), inc="ds", n=16)
        consts_ds = p.cnt["ds"]
        consts_dg = p.cnt["dg"]

        def ln_stats2(src_ap, sq_ap, sb, src_ready=(), sq_key=None, width=1024):
            for eng, sem, v in src_ready:
                W(eng, sem, v)
            WF("vector", ("stats", sb))
            DO("vector", lambda e: e.tensor_reduce(
                stats[:, sb:sb + 1], src_ap, AXX, ALU.add), inc="v")
            W("vector", "v", p.cnt["v"])
            DO("vector", lambda e: e.tensor_scalar(
                stats[:, sb + 1:sb + 2], stats[:, sb:sb + 1], 1.0 / width, None,
                ALU.mult), inc="v")
            m_v = p.cnt["v"]
            WF("scalar", ("stats_s", sb))
            if sq_key:
                WF("scalar", sq_key)
            DO("scalar", lambda e: e.activation(
                sq_ap, src_ap, AF.Square,
                accum_out=stats[:, sb + 2:sb + 3]), inc="s")
            sq_s = p.cnt["s"]
            if sq_key:
                SF(sq_key, "s", sq_s)
            W("vector", "s", sq_s)
            DO("vector", lambda e: e.tensor_scalar(
                stats[:, sb + 3:sb + 4], stats[:, sb + 2:sb + 3], 1.0 / width,
                EPS, ALU.mult, ALU.add), inc="v")
            W("vector", "v", p.cnt["v"])
            DO("vector", lambda e: e.tensor_mul(
                stats[:, sb + 4:sb + 5], stats[:, sb + 1:sb + 2],
                stats[:, sb + 1:sb + 2]), inc="v")
            W("vector", "v", p.cnt["v"])
            DO("vector", lambda e: e.scalar_tensor_tensor(
                stats[:, sb + 5:sb + 6], stats[:, sb + 4:sb + 5], -1.0,
                stats[:, sb + 3:sb + 4], ALU.mult, ALU.add), inc="v")
            W("vector", "v", p.cnt["v"])
            DO("vector", lambda e: e.reciprocal(
                stats[:, sb + 6:sb + 7], stats[:, sb + 5:sb + 6]), inc="v")
            rec_v = p.cnt["v"]
            W("scalar", "s", p.cnt["s"])
            W("scalar", "v", rec_v)
            DO("scalar", lambda e: e.activation(
                stats[:, sb + 7:sb + 8], stats[:, sb + 6:sb + 7], AF.Sqrt),
                inc="s")
            r_s = p.cnt["s"]
            SF(("stats_s", sb), "s", r_s)
            W("vector", "s", r_s)
            DO("vector", lambda e: e.tensor_scalar(
                stats[:, sb + 8:sb + 9], stats[:, sb + 1:sb + 2],
                stats[:, sb + 7:sb + 8], -1.0, ALU.mult, ALU.mult), inc="v")
            negmr_v = p.cnt["v"]
            return r_s, negmr_v

        def ln_apply(out_ap, src_ap, sb, r_s, negmr_v):
            W("scalar", "s", r_s)
            W("scalar", "v", negmr_v)
            DO("scalar", lambda e: e.activation(
                out_ap, src_ap, AF.Identity, scale=stats[:, sb + 7:sb + 8],
                bias=stats[:, sb + 8:sb + 9]), inc="s")
            return p.cnt["s"]

        with ExitStack() as sa:
            ae = sa.enter_context
            q_fm = ae(nc.sbuf_tensor("q_fm", [128, 8 * T], BF16))
            k_lat = ae(nc.sbuf_tensor("k_lat", [128, 2 * S], BF16))
            v_lat = ae(nc.sbuf_tensor("v_lat", [128, 2 * S], BF16))
            rwkbT = ae(nc.sbuf_tensor("rwkbT", [128, 2 * D], BF16))
            wkaTs = ae(nc.sbuf_tensor("wkaT_s", [128, 8 * LAT], BF16))
            wvaTs = ae(nc.sbuf_tensor("wvaT_s", [128, 8 * LAT], BF16))
            wvbTs = ae(nc.sbuf_tensor("wvbT_s", [128, 2 * D], BF16))
            qrow_all = ae(nc.sbuf_tensor("qrow_all", [16, T], BF16))
            ehs = ae(nc.sbuf_tensor("eh_s", [128, 128], BF16))

            q_fm_v = q_fm[:].rearrange("p (c t) -> p c t", c=8)
            k_lat_v = k_lat[:].rearrange("p (c t) -> p c t", c=2)
            v_lat_v = v_lat[:].rearrange("p (c t) -> p c t", c=2)
            rwkbT_v = rwkbT[:].rearrange("p (c t) -> p c t", c=2)
            wkaT_v = wkaTs[:].rearrange("p (c t) -> p c t", c=8)
            wvaT_v = wvaTs[:].rearrange("p (c t) -> p c t", c=8)
            wvbT_v = wvbTs[:].rearrange("p (c t) -> p c t", c=2)

            DO("gpsimd", lambda e: e.dma_start(
                wkaTs[:].rearrange("p (c t) -> p c t", c=8),
                wkaT_d.rearrange("(c p) t -> p c t", p=128)), inc="dg", n=16)
            DO("gpsimd", lambda e: e.dma_start(
                wvaTs[:].rearrange("p (c t) -> p c t", c=8),
                wvaT_d.rearrange("(c p) t -> p c t", p=128)), inc="dg", n=16)
            DO("gpsimd", lambda e: e.dma_start(
                wvbTs[:].rearrange("p (c t) -> p c t", c=2),
                wvbT_d.rearrange("(c p) t -> p c t", p=128)), inc="dg", n=16)
            DO("gpsimd", lambda e: e.dma_start(ehs[:], eh_d), inc="dg", n=16)
            attw_dg = p.cnt["dg"]

            with ExitStack() as s0:
                e0 = s0.enter_context
                xn1 = e0(nc.sbuf_tensor("xn1_fm", [128, 8 * S], BF16))
                xch = [e0(nc.sbuf_tensor(f"xch{i}", [128, D], F32))
                       for i in (0, 1, 2)]
                xnt = [e0(nc.sbuf_tensor(f"xnt{i}", [128, D], F32))
                       for i in (0, 1, 2)]
                wqch = [e0(nc.sbuf_tensor(f"wqch{i}", [128, D], BF16)) for i in (0, 1)]
                wkbs = e0(nc.sbuf_tensor("wkb_s", [128, 8 * LAT], BF16))
                rwqT = e0(nc.sbuf_tensor("rwqT", [128, 8 * D], BF16))
                xn1_v = xn1[:].rearrange("p (c t) -> p c t", c=8)
                rwqT_v = rwqT[:].rearrange("p (c t) -> p c t", c=8)
                wkb_v = wkbs[:].rearrange("p (c t) -> p c t", c=8)

                W("tensor", "ds", consts_ds)
                slots = [(ti, hh) for ti in range(4) for hh in range(2)]
                slot_i = [0]

                def next_slot():
                    ti, hh = slots[slot_i[0] % 8]
                    slot_i[0] += 1
                    return ps[ti][:, hh * 512:(hh + 1) * 512], ("sl", ti, hh)

                qsq1 = [s0.enter_context(
                    nc.sbuf_tensor(f"qsq1_{i}", [128, 512], BF16))
                    for i in (0, 1)]
                w3stg = [s0.enter_context(
                    nc.sbuf_tensor("w3stg0", [128, D], BF16))]

                def emit_w3_stage(ic):
                    i = 0
                    p.wfull("gpsimd", f"gw{i}")
                    DO("gpsimd", lambda e, ic=ic, i=i: e.dma_start(
                        w3stg[i][:], w3T_d[ic * 128:(ic + 1) * 128, :]),
                        inc=f"gw{i}", n=16)
                    p.wfull("gpsimd", f"gw{i}")
                    DO("gpsimd", lambda e, ic=ic, i=i: e.dma_start(
                        w3st_d[ic * 128:(ic + 1) * 128, :], w3stg[i][:]),
                        inc=f"gw{i}", n=16)

                def emit_ln_chunk(mc):
                    i = mc % 3
                    sb = i * 16
                    WF("sync", ("xchq", i))
                    DO("sync", lambda e, a=xch[i], m=mc: e.dma_start(
                        a[:], xf[m * 128:(m + 1) * 128, :]), inc=f"dl{i}", n=16)
                    ldv = p.cnt[f"dl{i}"]
                    r_s, negmr_v = ln_stats2(
                        xch[i][:], xnt[i][:], sb,
                        src_ready=(("vector", f"dl{i}", ldv),
                                   ("scalar", f"dl{i}", ldv)),
                        sq_key=("xnt", i))
                    ap_s = ln_apply(xnt[i][:], xch[i][:], sb, r_s, negmr_v)
                    SF(("xchq", i), "s", ap_s)
                    SF(("stats", sb), "s", ap_s)
                    W("tensor", "s", ap_s)
                    for g in range(2):
                        sl, key = next_slot()
                        WF("tensor", key)
                        for k in range(4):
                            DO("tensor", lambda e, sl=sl, b=xnt[i], g=g, k=k:
                               e.transpose(
                                   sl[:, k * 128:(k + 1) * 128],
                                   b[:, (g * 4 + k) * 128:(g * 4 + k + 1) * 128],
                                   ident[:]),
                               inc="t" if k == 3 else None)
                        if g == 1:
                            SF(("xnt", i), "t", p.cnt["t"])
                        W("vector", "t", p.cnt["t"])
                        DO("vector", lambda e, sl=sl, m=mc, g=g: e.tensor_copy(
                            xn1_v[:, g * 4:(g + 1) * 4, m * 128:(m + 1) * 128],
                            sl[:].rearrange("p (c t) -> p c t", c=4)), inc="v")
                        SF(key, "v", p.cnt["v"])

                def emit_rwq(c):
                    i = c % 2
                    WF("gpsimd", ("wqch", i))
                    DO("gpsimd", lambda e, a=wqch[i], c=c: e.dma_start(
                        a[:], wq_d[c * 128:(c + 1) * 128, :]), inc=f"gq{i}", n=16)
                    p.wfull("tensor", f"gq{i}")
                    for g in range(2):
                        sl, key = next_slot()
                        WF("tensor", key)
                        for mi in range(4):
                            DO("tensor", lambda e, sl=sl, b=wqch[i], g=g,
                               mi=mi, c=c: e.matmul(
                                   sl[:, mi * 128:(mi + 1) * 128],
                                   b[:, (g * 4 + mi) * 128:
                                     (g * 4 + mi + 1) * 128],
                                   rtc[:, c * 128:(c + 1) * 128],
                                   start=True, stop=True),
                               inc="t" if mi == 3 else None)
                        if g == 1:
                            SF(("wqch", i), "t", p.cnt["t"])
                        W("vector", "t", p.cnt["t"])
                        DO("vector", lambda e, sl=sl, c=c, g=g: e.tensor_copy(
                            rwqT_v[:, g * 4:(g + 1) * 4, c * 128:(c + 1) * 128],
                            sl[:].rearrange("p (m o) -> p m o", m=4)), inc="v")
                        SF(key, "v", p.cnt["v"])

                def emit_rwkb():
                    DO("gpsimd", lambda e: e.dma_start(
                        wkbs[:].rearrange("p (c t) -> p c t", c=8),
                        wkb_d.rearrange("(c p) t -> p c t", p=128)),
                        inc="dg", n=16)
                    p.wfull("tensor", "dg")
                    for lm in range(2):
                        for g in range(2):
                            sl, key = next_slot()
                            WF("tensor", key)
                            for cc in range(4):
                                c = g * 4 + cc
                                DO("tensor", lambda e, sl=sl, lm=lm, c=c,
                                   cc=cc: e.matmul(
                                       sl[:, cc * 128:(cc + 1) * 128],
                                       wkb_v[:, c, lm * 128:(lm + 1) * 128],
                                       rtc[:, c * 128:(c + 1) * 128],
                                       start=True, stop=True),
                                   inc="t" if cc == 3 else None)
                            W("vector", "t", p.cnt["t"])
                            DO("vector", lambda e, sl=sl, lm=lm, g=g:
                               e.tensor_copy(
                                   rwkbT_v[:, lm, g * 512:(g + 1) * 512],
                                   sl[:]), inc="v")
                            SF(key, "v", p.cnt["v"])

                xn1_ready = {}

                def proj(wview, dst_v, oc, nt, evac_eng):
                    sl, key = next_slot()
                    WF("tensor", key)
                    W("tensor", "v", xn1_ready[nt])
                    for ic in range(8):
                        DO("tensor", lambda e, sl=sl, w=wview, ic=ic, oc=oc,
                           nt=nt: e.matmul(
                               sl, w[:, ic, oc * 128:(oc + 1) * 128],
                               xn1_v[:, ic, nt * 512:(nt + 1) * 512],
                               start=(ic == 0), stop=(ic == 7)),
                           inc="t" if ic == 7 else None)
                    mm = p.cnt["t"]
                    W(evac_eng, "t", mm)
                    if evac_eng == "vector":
                        DO("vector", lambda e, sl=sl, dv=dst_v, oc=oc, nt=nt:
                           e.tensor_copy(dv[:, oc, nt * 512:(nt + 1) * 512], sl),
                           inc="v")
                        SF(key, "v", p.cnt["v"])
                    else:
                        DO("scalar", lambda e, sl=sl, dv=dst_v, oc=oc, nt=nt:
                           e.activation(dv[:, oc, nt * 512:(nt + 1) * 512], sl,
                                        AF.Identity), inc="s")
                        SF(key, "s", p.cnt["s"])

                def emit_qrow(nt):
                    sl, key = next_slot()
                    for c in range(8):
                        i = c % 2
                        WF("vector", ("qsq1", i))
                        DO("vector", lambda e, i=i, c=c, nt=nt: e.tensor_mul(
                            qsq1[i][:], q_fm_v[:, c, nt * 512:(nt + 1) * 512],
                            q_fm_v[:, c, nt * 512:(nt + 1) * 512]), inc="v")
                        sqv = p.cnt["v"]
                        W("tensor", "v", sqv)
                        if c == 0:
                            WF("tensor", key)
                        DO("tensor", lambda e, sl=sl, i=i, c=c: e.matmul(
                            sl[0:16, :], ehs[:, c * 16:(c + 1) * 16],
                            qsq1[i][:], start=(c == 0), stop=(c == 7)),
                           inc="t")
                        SF(("qsq1", i), "t", p.cnt["t"])
                    W("vector", "t", p.cnt["t"])
                    DO("vector", lambda e, sl=sl, nt=nt: e.tensor_copy(
                        qrow_all[:, nt * 512:(nt + 1) * 512], sl[0:16, :]),
                       inc="v")
                    SF(key, "v", p.cnt["v"])

                q_groups = [(oc, nt) for nt in range(4) for oc in range(8)]
                for mc in range(32):
                    emit_ln_chunk(mc)
                    emit_w3_stage(mc)
                    if mc % 4 == 3:
                        xn1_ready[mc // 4] = p.cnt["v"]
                        if mc == 3:
                            p.wfull("tensor", "dg")
                    if 8 <= mc < 16:
                        emit_rwq(mc - 8)
                    if mc == 16:
                        emit_rwkb()
                    if mc % 4 == 3:
                        nt = mc // 4
                        for lc in range(2):
                            proj(wkaT_v, k_lat_v, lc, nt, "vector")
                        for lc in range(2):
                            proj(wvaT_v, v_lat_v, lc, nt, "vector")
                    if mc == 17:
                        rwq_ready_v = p.cnt["v"]
                    if mc >= 18 and q_groups:
                        W("tensor", "v", rwq_ready_v)
                        for _ in range(3):
                            if q_groups:
                                oc, nt = q_groups.pop(0)
                                proj(rwqT_v, q_fm_v, oc, nt, "scalar")
                while q_groups:
                    oc, nt = q_groups.pop(0)
                    proj(rwqT_v, q_fm_v, oc, nt, "scalar")
                q_ready_s = p.cnt["s"]
                kv_ready_v = p.cnt["v"]
                for nt in range(4):
                    emit_qrow(nt)
                qrow_all_v = p.cnt["v"]


                p.barrier()
                p.emit_block()

            with ExitStack() as sb:
                be = sb.enter_context
                attnout = be(nc.sbuf_tensor("attnout", [128, 8 * T], BF16))
                woTs = be(nc.sbuf_tensor("woT_s", [128, 8 * D], BF16))
                woT_v = woTs[:].rearrange("p (c t) -> p c t", c=8)
                attnout_v = attnout[:].rearrange("p (c t) -> p c t", c=8)
                s2 = sb.enter_context(ExitStack())
                e2 = s2.enter_context
                k_aug = [e2(nc.sbuf_tensor(f"k_aug{i}", [128, S], BF16))
                         for i in (0, 1)]
                q_aug = [e2(nc.sbuf_tensor(f"q_aug{i}", [128, T], BF16))
                         for i in (0, 1)]
                v2h = [e2(nc.sbuf_tensor(f"v2h{i}", [128, 32 * 130], BF16))
                       for i in (0, 1)]
                pT = [e2(nc.sbuf_tensor(f"pT{i}", [128, 1024], BF16))
                      for i in (0, 1, 2)]
                sqscr = e2(nc.sbuf_tensor("sqscr", [128, 8 * 512], BF16))
                sqscr_v = sqscr[:].rearrange("p (c t) -> p c t", c=8)
                kmxs = e2(nc.sbuf_tensor("kmxs", [1, 16], F32))
                stage65 = e2(nc.sbuf_tensor("stage65", [1, T], BF16))
                recip = e2(nc.sbuf_tensor("recip", [128, 512], F32))
                recipbf = e2(nc.sbuf_tensor("recipbf", [128, 512], BF16))
                recipb = e2(nc.sbuf_tensor("recipb", [64, 512], F32))
                anorm = [e2(nc.sbuf_tensor(f"anorm{i}", [64, 512], BF16))
                         for i in (0, 1)]
                v2h_v = [b[:].rearrange("p (c t) -> p c t", c=32) for b in v2h]

                DO("gpsimd", lambda e: e.dma_start(
                    woTs[:].rearrange("p (c t) -> p c t", c=8),
                    woT_d.rearrange("(c p) t -> p c t", p=128)), inc="dg", n=16)
                wo_dg = p.cnt["dg"]

                for i in (0, 1):
                    DO("vector", lambda e, b=v2h_v[i]: e.memset(
                        b[:, :, 64:65], 1.0), inc="v")
                    DO("vector", lambda e, b=v2h_v[i]: e.memset(
                        b[:, :, 129:130], 1.0), inc="v")
                    DO("vector", lambda e, b=k_aug[i]: e.memset(
                        b[64:128, :], 0.0), inc="v")
                    W("vector", "v", p.cnt["v"])
                    DO("vector", lambda e, b=k_aug[i]: e.memset(
                        b[64:66, :], -1.0), inc="v")
                    DO("vector", lambda e, b=q_aug[i]: e.memset(
                        b[64:128, :], 0.0), inc="v")
                ones_init_v = p.cnt["v"]
                W("tensor", "v", ones_init_v)

                hstate = {}
                pending = []

                def stages_for(H):
                    hb = H % 2
                    hc, hp = H // 2, H % 2
                    st = []

                    def expand(nt):
                        sl_i = nt % 2
                        sl = ps[3][0:64, sl_i * 512:(sl_i + 1) * 512]
                        WF("tensor", ("pd", sl_i))
                        for lc in range(2):
                            DO("tensor", lambda e, sl=sl, lc=lc, nt=nt:
                               e.matmul(sl,
                                        rwkbT_v[:, lc, H * 64:(H + 1) * 64],
                                        k_lat_v[:, lc, nt * 512:(nt + 1) * 512],
                                        start=(lc == 0), stop=(lc == 1)),
                               inc="t" if lc == 1 else None)
                        mm = p.cnt["t"]
                        W("vector", "t", mm)
                        if nt == 0:
                            WF("vector", ("kaug", hb))
                        DO("vector", lambda e, sl=sl, nt=nt:
                           e.tensor_copy(k_aug[hb][0:64, nt * 512:(nt + 1) * 512],
                                         sl), inc="v")
                        SF(("pd", sl_i), "v", p.cnt["v"])
                        if nt == 0:
                            hstate.setdefault(H, {})
                        if nt == 7:
                            hstate[H]["kaug_v"] = p.cnt["v"]

                    def square(nt):
                        W("vector", "v", p.cnt["v"])
                        WF("vector", ("sqscr", nt))
                        DO("vector", lambda e, nt=nt: e.tensor_mul(
                            sqscr_v[0:64, nt, :],
                            k_aug[hb][0:64, nt * 512:(nt + 1) * 512],
                            k_aug[hb][0:64, nt * 512:(nt + 1) * 512]), inc="v")
                        hstate[H][("sq", nt)] = p.cnt["v"]

                    def sqmm(nt):
                        sl_i = nt % 2
                        sl = ps[3][0:1, sl_i * 512:(sl_i + 1) * 512]
                        W("tensor", "v", hstate[H][("sq", nt)])
                        WF("tensor", ("pd", sl_i))
                        DO("tensor", lambda e, sl=sl, nt=nt: e.matmul(
                            sl, onesc[0:64, :], sqscr_v[0:64, nt, :],
                            start=True, stop=True), inc="t")
                        SF(("sqscr", nt), "t", p.cnt["t"])
                        W("vector", "t", p.cnt["t"])
                        WF("vector", ("kmx", nt))
                        DO("vector", lambda e, sl=sl, nt=nt: e.tensor_reduce(
                            kmxs[0:1, nt:nt + 1], sl, AXX, ALU.max), inc="v")
                        SF(("pd", sl_i), "v", p.cnt["v"])
                        SF(("kmx", nt), "v", p.cnt["v"])

                    def final():
                        W("vector", "v", p.cnt["v"])
                        WF("vector", ("kmx", 8))
                        DO("vector", lambda e: e.tensor_reduce(
                            kmxs[0:1, 8:9], kmxs[0:1, 0:8], AXX, ALU.max),
                           inc="v")
                        W("vector", "v", p.cnt["v"])
                        DO("vector", lambda e: e.tensor_scalar(
                            kmxs[0:1, 10:11], kmxs[0:1, 8:9], 0.5, None,
                            ALU.mult), inc="v")
                        half_v = p.cnt["v"]
                        for nt in range(8):
                            SF(("kmx", nt), "v", half_v)
                        SF(("kmx", 8), "v", half_v)
                        W("vector", "v", half_v)
                        WF("vector", ("st65",))
                        DO("vector", lambda e: e.tensor_scalar(
                            stage65[0:1, :], qrow_all[0:1, :], 0.0,
                            kmxs[0:1, 10:11], ALU.mult, ALU.add), inc="v")
                        hstate[H]["st65_v"] = p.cnt["v"]
                        SF(("kmx", 8), "v", p.cnt["v"])

                    def dmas():
                        W("gpsimd", "s", q_ready_s)
                        W("gpsimd", "v", qrow_all_v)
                        W("gpsimd", "v", ones_init_v)
                        WF("gpsimd", ("qaug", hb))
                        qb = hp * 64
                        DO("gpsimd", lambda e, qb=qb: e.dma_start(
                            q_aug[hb][0:64, :], q_fm_v[qb:qb + 64, hc, :]),
                            inc=f"ga{hb}", n=16)
                        DO("gpsimd", lambda e: e.dma_start(
                            q_aug[hb][64:65, :], qrow_all[hc * 2 + hp:
                                                          hc * 2 + hp + 1, :]),
                            inc=f"ga{hb}", n=16)
                        W("gpsimd", "v", hstate[H]["st65_v"])
                        DO("gpsimd", lambda e: e.dma_start(
                            q_aug[hb][65:66, :], stage65[0:1, :]),
                            inc=f"ga{hb}", n=16)
                        SF(("st65",), f"ga{hb}", p.cnt[f"ga{hb}"])

                    def vgroup(g):
                        vb = (H // 2) % 2
                        sl_i = g % 2
                        sl = ps[3][:, sl_i * 512:(sl_i + 1) * 512]
                        WF("tensor", ("pd", sl_i))
                        if g == 0:
                            WF("tensor", ("v2h", vb))
                        for kc4 in range(4):
                            for lc in range(2):
                                DO("tensor", lambda e, sl=sl, kc4=kc4,
                                   lc=lc, g=g: e.matmul(
                                       sl[:, kc4 * 128:(kc4 + 1) * 128],
                                       v_lat_v[:, lc,
                                               (g * 4 + kc4) * 128:
                                               (g * 4 + kc4 + 1) * 128],
                                       wvbT_v[:, lc, H * 64:H * 64 + 128],
                                       start=(lc == 0), stop=(lc == 1)),
                                   inc="t" if (kc4 == 3 and lc == 1)
                                   else None)
                        mm = p.cnt["t"]
                        W("vector", "t", mm)
                        WF("vector", ("v2h", vb))
                        for hh in range(2):
                            DO("vector", lambda e, sl=sl, vb=vb, g=g, hh=hh:
                               e.tensor_copy(
                                   v2h_v[vb][:, g * 4:(g + 1) * 4,
                                             hh * 65:hh * 65 + 64],
                                   sl[:].rearrange(
                                       "p (c d) -> p c d", c=4)[:, :,
                                       hh * 64:(hh + 1) * 64]),
                               inc="v" if hh == 1 else None)
                        SF(("pd", sl_i), "v", p.cnt["v"])
                        if g == 7:
                            SF(("v2h_ready", vb), "v", p.cnt["v"])

                    st.append(lambda: (expand(0), expand(1)))
                    st.append(lambda: (expand(2), expand(3), square(0), square(1)))
                    st.append(lambda: (expand(4), expand(5), square(2), square(3)))
                    st.append(lambda: (expand(6), expand(7), square(4), square(5)))
                    st.append(lambda: (square(6), square(7), sqmm(0), sqmm(1)))
                    st.append(lambda: (sqmm(2), sqmm(3)))
                    st.append(lambda: (sqmm(4), sqmm(5)))
                    st.append(lambda: (sqmm(6), sqmm(7)))
                    st.append(final)
                    st.append(dmas)
                    if H % 2 == 0:
                        for g in range(8):
                            st.append(lambda g=g: vgroup(g))
                    return st

                exp_sg = {}

                def emit_av(g, need_wait=True):
                    h, j, b = g // 64, (g // 16) % 4, g % 16
                    hp = h % 2
                    vb = (h // 2) % 2
                    ji = (g // 16) % 2
                    av = ps[2][0:65, ji * 512:(ji + 1) * 512]
                    i = g % 3
                    if need_wait:
                        W("tensor", "s", exp_sg[g])
                    if b == 0:
                        WF("tensor", ("av", ji))
                        if ("v2h_ready", vb) in p.free:
                            WF("tensor", ("v2h_ready", vb))
                    for kk in range(2):
                        kc = 2 * b + kk
                        DO("tensor", lambda e, av=av, kc=kc, kk=kk,
                           i=i, vb=vb, hp=hp,
                           st=(b == 0 and kk == 0),
                           sp=(b == 15 and kk == 1): e.matmul(
                               av,
                               v2h_v[vb][:, kc, hp * 65:hp * 65 + 65],
                               pT[i][:, kk * 512:(kk + 1) * 512],
                               start=st, stop=sp),
                           inc="t" if kk == 1 else None)
                    SF(("pT", i), "t", p.cnt["t"])
                    if g % 64 == 63:
                        SF(("kaug", h % 2), "t", p.cnt["t"])
                        SF(("qaug", h % 2), "t", p.cnt["t"])
                        if h % 2 == 1:
                            SF(("v2h", (h // 2) % 2), "t", p.cnt["t"])

                def emit_epilogue(g):
                    h, j = g // 64, (g // 16) % 4
                    hc, hp = h // 2, h % 2
                    ji = (g // 16) % 2
                    W("vector", "t", p.cnt["t"])
                    WF("vector", ("recip",))
                    DO("vector", lambda e, ji=ji: e.reciprocal(
                        recip[64:65, :], ps[2][64:65, ji * 512:(ji + 1) * 512]),
                        inc="v")
                    W("vector", "v", p.cnt["v"])
                    WF("vector", ("recipbf",))
                    DO("vector", lambda e: e.tensor_copy(
                        recipbf[64:65, :], recip[64:65, :]), inc="v")
                    rc_v = p.cnt["v"]
                    SF(("recip",), "v", rc_v)

                    def epi1(rc_v=rc_v, ji=ji):
                        W("tensor", "v", rc_v)
                        WF("tensor", ("pd", ji))
                        DO("tensor", lambda e, ji=ji: e.matmul(
                            ps[3][0:64, ji * 512:(ji + 1) * 512],
                            ones128[64:65, :], recipbf[64:65, :],
                            start=True, stop=True), inc="t")
                        hstate[("bc", ji)] = p.cnt["t"]
                        SF(("recipbf",), "t", p.cnt["t"])

                    def epi2(h=h, j=j, ji=ji, hp=hp, hc=hc):
                        W("vector", "t", hstate[("bc", ji)])
                        WF("vector", ("recipb",))
                        DO("vector", lambda e, ji=ji: e.tensor_copy(
                            recipb[:], ps[3][0:64, ji * 512:(ji + 1) * 512]),
                            inc="v")
                        SF(("pd", ji), "v", p.cnt["v"])
                        W("vector", "v", p.cnt["v"])
                        ai = j % 2
                        WF("vector", ("anorm", ai))
                        DO("vector", lambda e, ai=ai, ji=ji: e.tensor_tensor(
                            anorm[ai][:], ps[2][0:64, ji * 512:(ji + 1) * 512],
                            recipb[:], ALU.mult), inc="v")
                        an_v = p.cnt["v"]
                        SF(("av", ji), "v", an_v)
                        SF(("recipb",), "v", an_v)
                        W("gpsimd", "v", an_v)
                        DO("gpsimd", lambda e, ai=ai, hp=hp, hc=hc, j=j:
                           e.dma_start(
                               attnout_v[hp * 64:hp * 64 + 64, hc,
                                         j * 512:(j + 1) * 512],
                               anorm[ai][:]), inc=f"gn{ai}", n=16)
                        SF(("anorm", ai), f"gn{ai}", p.cnt[f"gn{ai}"])

                    pos = min(2, len(pending))
                    pending[pos:pos] = [epi1, epi2]

                for fn in stages_for(0):
                    fn()
                G = NH * 64
                for g in range(G):
                    h, j, b = g // 64, (g // 16) % 4, g % 16
                    hb = h % 2
                    sc_i = g % 2
                    pt_i = g % 3
                    sc = ps[sc_i]
                    if g % 64 == 0 and h + 1 < NH:
                        pending.extend(stages_for(h + 1))
                    if g >= 2:
                        W("tensor", "s", exp_sg[g - 2])
                    if g % 64 == 0:
                        W("tensor", "v", hstate[h]["kaug_v"])
                        p.wfull("tensor", f"ga{hb}")
                    for kk in range(2):
                        kc = 2 * b + kk
                        DO("tensor", lambda e, sc=sc, kk=kk, kc=kc,
                           hb=hb, j=j: e.matmul(
                               sc[:, kk * 512:(kk + 1) * 512],
                               k_aug[hb][0:128, kc * 128:(kc + 1) * 128],
                               q_aug[hb][0:128, j * 512:(j + 1) * 512],
                               start=True, stop=True),
                           inc="t" if kk == 1 else None)
                    sc_t = p.cnt["t"]
                    W("scalar", "t", sc_t)
                    WF("scalar", ("pT", pt_i))
                    DO("scalar", lambda e, sc=sc, i=pt_i: e.activation(
                        pT[i][:], sc[:, 0:1024], AF.Exp, scale=0.125),
                        inc="s")
                    exp_sg[g] = p.cnt["s"]
                    SF(("sc", sc_i), "s", exp_sg[g])
                    if g >= 2:
                        emit_av(g - 2, need_wait=False)
                        if (g - 2) % 16 == 15:
                            emit_epilogue(g - 2)
                    if b % 2 == 1 and pending:
                        pending.pop(0)()
                emit_av(G - 2)
                emit_av(G - 1)
                emit_epilogue(G - 1)
                while pending:
                    pending.pop(0)()

                attn_done_dg = p.cnt["dg"]
                p.barrier()
                p.emit_block()
                s2.close()

                with ExitStack() as s3:
                    e3 = s3.enter_context
                    xoch = [e3(nc.sbuf_tensor(f"xoch{i}", [128, D], F32))
                            for i in (0, 1)]
                    t3 = [e3(nc.sbuf_tensor(f"t3_{i}", [128, D], F32))
                          for i in (0, 1)]
                    x1c = [e3(nc.sbuf_tensor(f"x1c{i}", [128, D], F32))
                           for i in (0, 1)]
                    p.wfull("tensor", "gn0")
                    p.wfull("tensor", "gn1")
                    p.wfull("tensor", "dg")
                    for mc in range(16):
                        i = mc % 2
                        pp = ps[mc % 4]
                        WF("sync", ("xoch", i))
                        DO("sync", lambda e, a=xoch[i], m=mc: e.dma_start(
                            a[:], xf[m * 128:(m + 1) * 128, :]), inc=f"dl{i}",
                            n=16)
                        ldxv = p.cnt[f"dl{i}"]
                        WF("tensor", ("p3", mc % 4))
                        for nt in range(2):
                            for hcc in range(8):
                                DO("tensor", lambda e, a=pp, nt=nt, hcc=hcc,
                                   m=mc: e.matmul(
                                       a[:, nt * 512:(nt + 1) * 512],
                                       attnout_v[:, hcc, m * 128:(m + 1) * 128],
                                       woT_v[:, hcc, nt * 512:(nt + 1) * 512],
                                       start=(hcc == 0), stop=(hcc == 7)),
                                   inc="t" if (hcc == 7 and nt == 1) else None)
                        mm = p.cnt["t"]
                        sb = i * 16
                        r_s, negmr_v = ln_stats2(
                            pp[:, 0:1024], t3[i][:], sb,
                            src_ready=(("vector", "t", mm), ("scalar", "t", mm)),
                            sq_key=("t3", i))
                        ap_s = ln_apply(t3[i][:], pp[:, 0:1024], sb, r_s,
                                        negmr_v)
                        SF(("p3", mc % 4), "s", ap_s)
                        SF(("stats", sb), "s", ap_s)
                        W("vector", "s", ap_s)
                        W("vector", f"dl{i}", ldxv)
                        WF("vector", ("x1c", i))
                        DO("vector", lambda e, i=i: e.tensor_add(
                            x1c[i][:], t3[i][:], xoch[i][:]), inc="v")
                        x1_v = p.cnt["v"]
                        SF(("t3", i), "v", x1_v)
                        SF(("xoch", i), "v", x1_v)
                        W("sync", "v", x1_v)
                        DO("sync", lambda e, i=i, m=mc: e.dma_start(
                            x1sp_d[m * 128:(m + 1) * 128, :], x1c[i][:]),
                            inc=f"dst{i}", n=16)
                        SF(("x1c", i), f"dst{i}", p.cnt[f"dst{i}"])
                    p.barrier()
                    p.emit_block()

        with ExitStack() as s4:
            e4 = s4.enter_context
            abuf = e4(nc.sbuf_tensor("abuf", [128, 32 * 1024], BF16))
            bbuf = e4(nc.sbuf_tensor("bbuf", [128, 32 * 1024], BF16))
            x1c4 = [e4(nc.sbuf_tensor(f"x1c4_{i}", [128, D], F32))
                    for i in (0, 1)]
            xnt4 = [e4(nc.sbuf_tensor(f"xnt4_{i}", [128, D], F32))
                    for i in (0, 1)]
            silu_sb = [e4(nc.sbuf_tensor(f"silu{i}", [128, 1024], F32))
                       for i in (0, 1)]
            w1t = [e4(nc.sbuf_tensor(f"w1t{i}", [128, 8 * 256], BF16))
                   for i in (0, 1)]
            w2t = [e4(nc.sbuf_tensor(f"w2t{i}", [128, 8 * 256], BF16))
                   for i in (0, 1)]
            h1t = [e4(nc.sbuf_tensor(f"h1t{i}", [128, 32 * 128], BF16))
                   for i in (0, 1)]
            tf3 = [e4(nc.sbuf_tensor(f"tf3_{i}", [128, D], F32))
                   for i in (0, 1)]
            a_v = abuf[:].rearrange("p (c t) -> p c t", c=32)
            b_v = bbuf[:].rearrange("p (c t) -> p c t", c=32)
            w1t_v = [w[:].rearrange("p (c t) -> p c t", c=8) for w in w1t]
            w2t_v = [w[:].rearrange("p (c t) -> p c t", c=8) for w in w2t]
            h1t_v = [w[:].rearrange("p (c t) -> p c t", c=32) for w in h1t]

            for tt in range(2):
                for mc in range(8):
                    i = mc % 2
                    pp = ps[mc % 4]
                    row0 = tt * 1024 + mc * 128
                    WF("sync", ("x1c4", i))
                    p.wfull("sync", "dst0")
                    p.wfull("sync", "dst1")
                    DO("sync", lambda e, a=x1c4[i], r=row0: e.dma_start(
                        a[:], x1sp_d[r:r + 128, :]), inc=f"dl{i}", n=16)
                    ldv = p.cnt[f"dl{i}"]
                    sb = i * 16
                    r_s, negmr_v = ln_stats2(
                        x1c4[i][:], xnt4[i][:], sb,
                        src_ready=(("vector", f"dl{i}", ldv),
                                   ("scalar", f"dl{i}", ldv)),
                        sq_key=("xnt4", i))
                    ap_s = ln_apply(xnt4[i][:], x1c4[i][:], sb, r_s, negmr_v)
                    SF(("x1c4", i), "s", ap_s)
                    SF(("stats", sb), "s", ap_s)
                    W("tensor", "s", ap_s)
                    WF("tensor", ("p4a", mc % 4))
                    for k in range(8):
                        DO("tensor", lambda e, a=pp, b=xnt4[i], k=k: e.transpose(
                            a[:, k * 128:(k + 1) * 128],
                            b[:, k * 128:(k + 1) * 128], ident[:]),
                           inc="t" if k == 7 else None)
                    SF(("xnt4", i), "t", p.cnt["t"])
                    W("vector", "t", p.cnt["t"])
                    WF("vector", ("bxn", mc))
                    DO("vector", lambda e, a=pp, m=mc: e.tensor_copy(
                        b_v[:, 0:8, m * 128:(m + 1) * 128],
                        a[:].rearrange("p (c t) -> p c t", c=8)), inc="v")
                    SF(("p4a", i), "v", p.cnt["v"])
                xn2_v_ready = p.cnt["v"]

                W("tensor", "v", xn2_v_ready)
                for ob in range(16):
                    i = ob % 2
                    WF("gpsimd", ("w1t", i))
                    WF("gpsimd", ("w2t", i))
                    DO("gpsimd", lambda e, a=w1t[i], ob=ob: e.dma_start(
                        a[:].rearrange("p (c t) -> p c t", c=8), w1b_d[ob]),
                        inc=f"gw{i}", n=16)
                    DO("gpsimd", lambda e, a=w2t[i], ob=ob: e.dma_start(
                        a[:].rearrange("p (c t) -> p c t", c=8), w2b_d[ob]),
                        inc=f"gw{i}", n=16)
                    for oc2 in range(2):
                        oc = ob * 2 + oc2
                        pi = oc % 2
                        pX, pY = ps[2 * pi], ps[2 * pi + 1]
                        p.wfull("tensor", f"gw{i}")
                        WF("tensor", ("pB", pi))
                        for nt in range(2):
                            for ic in range(8):
                                DO("tensor", lambda e, a=pX, nt=nt, ic=ic,
                                   i=i, oc2=oc2: e.matmul(
                                       a[:, nt * 512:(nt + 1) * 512],
                                       w1t_v[i][:, ic,
                                               oc2 * 128:(oc2 + 1) * 128],
                                       b_v[:, ic, nt * 512:(nt + 1) * 512],
                                       start=(ic == 0), stop=(ic == 7)),
                                   inc="t" if (ic == 7 and nt == 1) else None)
                        mmx = p.cnt["t"]
                        for nt in range(2):
                            for ic in range(8):
                                DO("tensor", lambda e, a=pY, nt=nt, ic=ic,
                                   i=i, oc2=oc2: e.matmul(
                                       a[:, nt * 512:(nt + 1) * 512],
                                       w2t_v[i][:, ic,
                                               oc2 * 128:(oc2 + 1) * 128],
                                       b_v[:, ic, nt * 512:(nt + 1) * 512],
                                       start=(ic == 0), stop=(ic == 7)),
                                   inc="t" if (ic == 7 and nt == 1) else None)
                        mmy = p.cnt["t"]
                        W("scalar", "t", mmx)
                        WF("scalar", ("silu", pi))
                        DO("scalar", lambda e, a=pX, pi=pi, oc=oc: e.activation(
                            silu_sb[pi][:], a[:, 0:1024], AF.Silu,
                            scale=s1s[:, oc:oc + 1]), inc="s")
                        sil_s = p.cnt["s"]
                        W("vector", "t", mmy)
                        W("vector", "s", sil_s)
                        WF("vector", ("a_ch", oc))
                        DO("vector", lambda e, a=pY, pi=pi, oc=oc:
                           e.scalar_tensor_tensor(
                               a_v[:, oc, :], a[:, 0:1024], s2s[:, oc:oc + 1],
                               silu_sb[pi][:], ALU.mult, ALU.mult), inc="v")
                        av_ = p.cnt["v"]
                        SF(("silu", pi), "v", av_)
                        SF(("pB", pi), "v", av_)
                        SF(("pBx", pi), "s", sil_s)
                    SF(("w1t", i), "t", p.cnt["t"])
                    SF(("w2t", i), "t", p.cnt["t"])
                a_ready_v = p.cnt["v"]

                for sub, src_v, dst_v, wdram, ss in (
                        ("C", a_v, b_v, h1b_d, sh1s), ("D", b_v, a_v, h2b_d, sh2s)):
                    if sub == "C":
                        W("tensor", "v", a_ready_v)
                    for oc in range(32):
                        i = oc % 2
                        WF("gpsimd", ("h1t", i))
                        DO("gpsimd", lambda e, a=h1t[i], oc=oc, w=wdram:
                                e.dma_start(a[:].rearrange(
                                    "p (c t) -> p c t", c=32), w[oc]),
                                inc=f"gh{i}", n=16)
                        pi = oc % 2
                        pp = ps[2 + pi]
                        p.wfull("tensor", f"gh{i}")
                        WF("tensor", ("pCD", pi))
                        for nt in range(2):
                            for ic in range(32):
                                DO("tensor", lambda e, a=pp, nt=nt, ic=ic, i=i,
                                   sv=src_v: e.matmul(
                                       a[:, nt * 512:(nt + 1) * 512],
                                       h1t_v[i][:, ic, :],
                                       sv[:, ic, nt * 512:(nt + 1) * 512],
                                       start=(ic == 0), stop=(ic == 31)),
                                   inc="t" if (ic == 31 and nt == 1) else None)
                        mm = p.cnt["t"]
                        SF(("h1t", i), "t", mm)
                        W("scalar", "t", mm)
                        WF("scalar", ("cd_ch", sub, oc))
                        DO("scalar", lambda e, a=pp, oc=oc, dv=dst_v, ss=ss:
                           e.activation(dv[:, oc, :], a[:, 0:1024], AF.Gelu,
                                        scale=ss[:, oc:oc + 1]), inc="s")
                        SF(("pCD", pi), "s", p.cnt["s"])
                        SF(("cd_in", sub, oc), "s", p.cnt["s"])
                    cd_s = p.cnt["s"]
                    if sub == "C":
                        W("tensor", "s", cd_s)
                c_ready_s = p.cnt["s"]

                W("sync", "s", c_ready_s)
                W("sync", "t", p.cnt["t"])
                for ic in range(32):
                    DO("sync", lambda e, ic=ic: e.dma_start(
                        b_v[:, ic, :], w3st_d[ic * 128:(ic + 1) * 128, :]),
                        inc="ds", n=16)
                p.wfull("tensor", "ds")
                W("tensor", "s", c_ready_s)
                for mc in range(8):
                    i = mc % 2
                    pp = ps[mc % 4]
                    row0 = tt * 1024 + mc * 128
                    WF("sync", ("x1c4", i))
                    DO("sync", lambda e, a=x1c4[i], r=row0: e.dma_start(
                        a[:], x1sp_d[r:r + 128, :]), inc=f"dl{i}", n=16)
                    ldv = p.cnt[f"dl{i}"]
                    WF("tensor", ("p4e", mc % 4))
                    for nt in range(2):
                        for ic in range(32):
                            DO("tensor", lambda e, a=pp, nt=nt, ic=ic, m=mc:
                               e.matmul(
                                   a[:, nt * 512:(nt + 1) * 512],
                                   a_v[:, ic, m * 128:(m + 1) * 128],
                                   b_v[:, ic, nt * 512:(nt + 1) * 512],
                                   start=(ic == 0), stop=(ic == 31)),
                               inc="t" if (ic == 31 and nt == 1) else None)
                    mm = p.cnt["t"]
                    W("vector", "t", mm)
                    WF("vector", ("tf3", i))
                    DO("vector", lambda e, a=pp, i=i: e.tensor_tensor(
                        tf3[i][:], a[:, 0:1024], s3bs[:], ALU.mult), inc="v")
                    f_v = p.cnt["v"]
                    SF(("p4e", mc % 4), "v", f_v)
                    sb = i * 16
                    r_s, negmr_v = ln_stats2(
                        tf3[i][:], xnt4[i][:], sb,
                        src_ready=(("vector", "v", f_v), ("scalar", "v", f_v)),
                        sq_key=("xnt4", i))
                    ap_s = ln_apply(xnt4[i][:], tf3[i][:], sb, r_s, negmr_v)
                    SF(("stats", sb), "s", ap_s)
                    W("vector", "s", ap_s)
                    W("vector", f"dl{i}", ldv)
                    DO("vector", lambda e, i=i: e.tensor_add(
                        tf3[i][:], xnt4[i][:], x1c4[i][:]), inc="v")
                    o_v = p.cnt["v"]
                    SF(("xnt4", i), "v", o_v)
                    SF(("x1c4", i), "v", o_v)
                    W("sync", "v", o_v)
                    DO("sync", lambda e, i=i, r=row0: e.dma_start(
                        out_d[r:r + 128, :], tf3[i][:]), inc=f"dst{i}", n=16)
                    SF(("tf3", i), f"dst{i}", p.cnt[f"dst{i}"])
                W("gpsimd", "t", p.cnt["t"])

            p.barrier()
            p.emit_block()

    return nc


def _host_prep(L0_attn_wq, L0_attn_wk_a, L0_attn_wk_b, L0_attn_wv_a,
               L0_attn_wv_b, L0_attn_wo, L0_ffn_w1_w, L0_ffn_w1_s,
               L0_ffn_w2_w, L0_ffn_w2_s, L0_ffn_h1_w, L0_ffn_h1_s,
               L0_ffn_h2_w, L0_ffn_h2_s, L0_ffn_w3_w, L0_ffn_w3_s):
    f = np.float32
    C = np.ascontiguousarray

    R = np.eye(D, dtype=np.float64)
    inv = 10000.0 ** (-(np.arange(0, 16, 2, dtype=np.float64)) / HD)
    for h in range(NH):
        for j in range(8):
            ang = h * inv[j]
            c_, s_ = np.cos(ang), np.sin(ang)
            b0 = h * HD
            R[b0 + 2 * j, b0 + 2 * j] = c_
            R[b0 + 2 * j, b0 + 2 * j + 1] = -s_
            R[b0 + 2 * j + 1, b0 + 2 * j] = s_
            R[b0 + 2 * j + 1, b0 + 2 * j + 1] = c_
    R = R.astype(f)
    rtc = np.zeros((128, 8 * 128), dtype=f)
    for c in range(8):
        rtc[:, c * 128:(c + 1) * 128] = R[c * 128:(c + 1) * 128,
                                          c * 128:(c + 1) * 128].T

    eh = np.zeros((128, 128), dtype=f)
    for c in range(8):
        for h_ in range(NH):
            if h_ // 2 == c:
                rows = slice((h_ % 2) * 64, (h_ % 2) * 64 + 64)
                eh[rows, c * 16 + h_] = 0.5

    w1T = L0_ffn_w1_w.T
    w2T = L0_ffn_w2_w.T
    h1T = L0_ffn_h1_w.T
    h2T = L0_ffn_h2_w.T
    ins = {
        "wq": C(L0_attn_wq.astype(f)),
        "wkb": C(L0_attn_wk_b.astype(f)),
        "wkaT": C(L0_attn_wk_a.T.astype(f)),
        "wvaT": C(L0_attn_wv_a.T.astype(f)),
        "wvbT": C(L0_attn_wv_b.T.astype(f)),
        "woT": C(L0_attn_wo.T.astype(f)),
        "w1b": C(w1T.reshape(8, 128, 16, 256).transpose(2, 1, 0, 3).astype(f)),
        "w2b": C(w2T.reshape(8, 128, 16, 256).transpose(2, 1, 0, 3).astype(f)),
        "h1b": C(h1T.reshape(32, 128, 32, 128).transpose(2, 1, 0, 3).astype(f)),
        "h2b": C(h2T.reshape(32, 128, 32, 128).transpose(2, 1, 0, 3).astype(f)),
        "w3T": C(L0_ffn_w3_w.T.astype(f)),
        "s1": C(L0_ffn_w1_s.reshape(32, 128).T.astype(f)),
        "s2": C(L0_ffn_w2_s.reshape(32, 128).T.astype(f)),
        "sh1": C(L0_ffn_h1_s.reshape(32, 128).T.astype(f)),
        "sh2": C(L0_ffn_h2_s.reshape(32, 128).T.astype(f)),
        "s3b": C(np.broadcast_to(L0_ffn_w3_s.astype(f), (128, D))),
        "ident": np.eye(128, dtype=f),
        "eh": eh,
        "rtc": rtc,
        "ones128": np.ones((128, 64), dtype=f),
        "onesc": np.ones((128, 1), dtype=f),
    }
    return ins


def kernel(L0_attn_x, L0_attn_wq, L0_attn_wk_a, L0_attn_wk_b, L0_attn_wv_a,
           L0_attn_wv_b, L0_attn_wo, L0_ffn_w1_w, L0_ffn_w1_s, L0_ffn_w2_w,
           L0_ffn_w2_s, L0_ffn_h1_w, L0_ffn_h1_s, L0_ffn_h2_w, L0_ffn_h2_s,
           L0_ffn_w3_w, L0_ffn_w3_s):
    x = np.asarray(L0_attn_x, dtype=np.float32)
    shared = _host_prep(
        np.asarray(L0_attn_wq), np.asarray(L0_attn_wk_a),
        np.asarray(L0_attn_wk_b), np.asarray(L0_attn_wv_a),
        np.asarray(L0_attn_wv_b), np.asarray(L0_attn_wo),
        np.asarray(L0_ffn_w1_w), np.asarray(L0_ffn_w1_s),
        np.asarray(L0_ffn_w2_w), np.asarray(L0_ffn_w2_s),
        np.asarray(L0_ffn_h1_w), np.asarray(L0_ffn_h1_s),
        np.asarray(L0_ffn_h2_w), np.asarray(L0_ffn_h2_s),
        np.asarray(L0_ffn_w3_w), np.asarray(L0_ffn_w3_s))

    nc = build_nc()
    in_maps = []
    for c in range(8):
        b, hf = c // 2, c % 2
        own = x[b, hf * T:(hf + 1) * T]
        oth = x[b, (1 - hf) * T:(2 - hf) * T]
        m = dict(shared)
        m["xf"] = np.ascontiguousarray(np.concatenate([own, oth], axis=0))
        in_maps.append(m)

    res = run_bass_kernel_spmd(nc, in_maps, core_ids=list(range(8)))
    out = np.empty((4, S, D), dtype=np.float32)
    for c in range(8):
        b, hf = c // 2, c % 2
        out[b, hf * T:(hf + 1) * T] = res.results[c]["out"]
    return out


# version 16
# speedup vs baseline: 1.0366x; 1.0366x over previous
import numpy as np
import concourse.bass as bass
import concourse.mybir as mybir
from concourse.bass_utils import run_bass_kernel_spmd
from contextlib import ExitStack

F32 = mybir.dt.float32
BF16 = mybir.dt.bfloat16
AF = mybir.ActivationFunctionType
ALU = mybir.AluOpType
AXX = mybir.AxisListType.X

D = 1024
S = 4096
T = 2048
LAT = 256
NH = 16
HD = 64
DH = 4096
EPS = 1e-5

ENGINES = ("tensor", "vector", "scalar", "sync", "gpsimd")


class Prog:
    def __init__(self, nc):
        self.nc = nc
        self.ops = {k: [] for k in ENGINES}
        self.sem = {}
        self.cnt = {}
        self.free = {}

    def add_sem(self, stack, name):
        self.sem[name] = stack.enter_context(self.nc.semaphore(name))
        self.cnt[name] = 0

    def do(self, eng, fn, inc=None, n=1):
        if inc is None:
            self.ops[eng].append(fn)
            return None
        self.cnt[inc] += n
        s = self.sem[inc]
        self.ops[eng].append(lambda e, f=fn, s=s, n=n: f(e).then_inc(s, n))
        return self.cnt[inc]

    def wait(self, eng, sem, val):
        assert val <= self.cnt[sem], (sem, val, self.cnt[sem])
        if val <= 0:
            return
        s = self.sem[sem]
        self.ops[eng].append(lambda e, s=s, v=val: e.wait_ge(s, v))

    def wfull(self, eng, sem):
        self.wait(eng, sem, self.cnt[sem])

    def wait_free(self, eng, key):
        if key in self.free:
            sname, v = self.free[key]
            self.wait(eng, sname, v)

    def set_free(self, key, sname, v):
        self.free[key] = (sname, v)

    def barrier(self):
        tgt = {k: self.cnt[k] for k in self.sem}
        for eng in ENGINES:
            for sname, v in tgt.items():
                self.wait(eng, sname, v)

    def emit_block(self):
        nc = self.nc
        ops = self.ops
        with nc.Block() as blk:
            @blk.tensor
            def _(e):
                for op in ops["tensor"]:
                    op(e)

            @blk.vector
            def _(e):
                for op in ops["vector"]:
                    op(e)

            @blk.scalar
            def _(e):
                for op in ops["scalar"]:
                    op(e)

            @blk.sync
            def _(e):
                for op in ops["sync"]:
                    op(e)

            @blk.gpsimd
            def _(e):
                for op in ops["gpsimd"]:
                    op(e)
        self.ops = {k: [] for k in ENGINES}


def build_nc():
    nc = bass.Bass("TRN2", target_bir_lowering=False)

    xf = nc.dram_tensor("xf", [S, D], F32, kind="ExternalInput").ap()
    wq_d = nc.dram_tensor("wq", [D, D], F32, kind="ExternalInput").ap()
    wkb_d = nc.dram_tensor("wkb", [D, LAT], F32, kind="ExternalInput").ap()
    wkaT_d = nc.dram_tensor("wkaT", [D, LAT], F32, kind="ExternalInput").ap()
    wvaT_d = nc.dram_tensor("wvaT", [D, LAT], F32, kind="ExternalInput").ap()
    wvbT_d = nc.dram_tensor("wvbT", [LAT, D], F32, kind="ExternalInput").ap()
    woT_d = nc.dram_tensor("woT", [D, D], F32, kind="ExternalInput").ap()
    w1b_d = nc.dram_tensor("w1b", [16, 128, 8, 256], F32, kind="ExternalInput").ap()
    w2b_d = nc.dram_tensor("w2b", [16, 128, 8, 256], F32, kind="ExternalInput").ap()
    h1b_d = nc.dram_tensor("h1b", [32, 128, 32, 128], F32, kind="ExternalInput").ap()
    h2b_d = nc.dram_tensor("h2b", [32, 128, 32, 128], F32, kind="ExternalInput").ap()
    w3T_d = nc.dram_tensor("w3T", [DH, D], F32, kind="ExternalInput").ap()
    s1_d = nc.dram_tensor("s1", [128, 32], F32, kind="ExternalInput").ap()
    s2_d = nc.dram_tensor("s2", [128, 32], F32, kind="ExternalInput").ap()
    sh1_d = nc.dram_tensor("sh1", [128, 32], F32, kind="ExternalInput").ap()
    sh2_d = nc.dram_tensor("sh2", [128, 32], F32, kind="ExternalInput").ap()
    s3b_d = nc.dram_tensor("s3b", [128, D], F32, kind="ExternalInput").ap()
    ident_d = nc.dram_tensor("ident", [128, 128], F32, kind="ExternalInput").ap()
    rtc_d = nc.dram_tensor("rtc", [128, 8 * 128], F32, kind="ExternalInput").ap()
    ones128_d = nc.dram_tensor("ones128", [128, 64], F32, kind="ExternalInput").ap()
    onesc_d = nc.dram_tensor("onesc", [128, 1], F32, kind="ExternalInput").ap()
    eh_d = nc.dram_tensor("eh", [128, 128], F32, kind="ExternalInput").ap()
    out_d = nc.dram_tensor("out", [T, D], F32, kind="ExternalOutput").ap()
    x1sp_d = nc.dram_tensor("x1sp", [T, D], F32, kind="Internal").ap()
    w3st_d = nc.dram_tensor("w3st", [DH, D], BF16, kind="Internal").ap()

    with ExitStack() as outer:
        oe = outer.enter_context
        p = Prog(nc)
        for sname in ("dg", "ds", "t", "v", "s", "dl0", "dl1", "dl2",
                      "dst0", "dst1",
                      "gq0", "gq1", "ga0", "ga1", "gn0", "gn1", "gw0", "gw1",
                      "gh0", "gh1"):
            p.add_sem(outer, sname)

        ident = oe(nc.sbuf_tensor("ident_s", [128, 128], F32))
        rtc = oe(nc.sbuf_tensor("rtc_s", [128, 8 * 128], BF16))
        ones128 = oe(nc.sbuf_tensor("ones128_s", [128, 64], BF16))
        onesc = oe(nc.sbuf_tensor("onesc_s", [128, 1], BF16))
        s1s = oe(nc.sbuf_tensor("s1_s", [128, 32], F32))
        s2s = oe(nc.sbuf_tensor("s2_s", [128, 32], F32))
        sh1s = oe(nc.sbuf_tensor("sh1_s", [128, 32], F32))
        sh2s = oe(nc.sbuf_tensor("sh2_s", [128, 32], F32))
        s3bs = oe(nc.sbuf_tensor("s3b_s", [128, D], F32))
        stats = oe(nc.sbuf_tensor("stats", [128, 64], F32))
        ps = [oe(nc.psum_tensor(f"ps{i}", [128, 1024], F32)) for i in range(4)]

        DO, W, WF, SF = p.do, p.wait, p.wait_free, p.set_free

        DO("sync", lambda e: e.dma_start(ident[:], ident_d), inc="ds", n=16)
        DO("gpsimd", lambda e: e.dma_start(rtc[:], rtc_d), inc="dg", n=16)
        DO("gpsimd", lambda e: e.dma_start(ones128[:], ones128_d), inc="dg", n=16)
        DO("gpsimd", lambda e: e.dma_start(onesc[:], onesc_d), inc="dg", n=16)
        for dst, src in ((s1s, s1_d), (s2s, s2_d), (sh1s, sh1_d), (sh2s, sh2_d),
                         (s3bs, s3b_d)):
            DO("sync", lambda e, d=dst, s_=src: e.dma_start(d[:], s_), inc="ds", n=16)
        consts_ds = p.cnt["ds"]
        consts_dg = p.cnt["dg"]

        def ln_stats2(src_ap, sq_ap, sb, src_ready=(), sq_key=None, width=1024):
            for eng, sem, v in src_ready:
                W(eng, sem, v)
            WF("vector", ("stats", sb))
            DO("vector", lambda e: e.tensor_reduce(
                stats[:, sb:sb + 1], src_ap, AXX, ALU.add), inc="v")
            W("vector", "v", p.cnt["v"])
            DO("vector", lambda e: e.tensor_scalar(
                stats[:, sb + 1:sb + 2], stats[:, sb:sb + 1], 1.0 / width, None,
                ALU.mult), inc="v")
            m_v = p.cnt["v"]
            WF("scalar", ("stats_s", sb))
            if sq_key:
                WF("scalar", sq_key)
            DO("scalar", lambda e: e.activation(
                sq_ap, src_ap, AF.Square,
                accum_out=stats[:, sb + 2:sb + 3]), inc="s")
            sq_s = p.cnt["s"]
            if sq_key:
                SF(sq_key, "s", sq_s)
            W("vector", "s", sq_s)
            DO("vector", lambda e: e.tensor_scalar(
                stats[:, sb + 3:sb + 4], stats[:, sb + 2:sb + 3], 1.0 / width,
                EPS, ALU.mult, ALU.add), inc="v")
            W("vector", "v", p.cnt["v"])
            DO("vector", lambda e: e.tensor_mul(
                stats[:, sb + 4:sb + 5], stats[:, sb + 1:sb + 2],
                stats[:, sb + 1:sb + 2]), inc="v")
            W("vector", "v", p.cnt["v"])
            DO("vector", lambda e: e.scalar_tensor_tensor(
                stats[:, sb + 5:sb + 6], stats[:, sb + 4:sb + 5], -1.0,
                stats[:, sb + 3:sb + 4], ALU.mult, ALU.add), inc="v")
            W("vector", "v", p.cnt["v"])
            DO("vector", lambda e: e.reciprocal(
                stats[:, sb + 6:sb + 7], stats[:, sb + 5:sb + 6]), inc="v")
            rec_v = p.cnt["v"]
            W("scalar", "s", p.cnt["s"])
            W("scalar", "v", rec_v)
            DO("scalar", lambda e: e.activation(
                stats[:, sb + 7:sb + 8], stats[:, sb + 6:sb + 7], AF.Sqrt),
                inc="s")
            r_s = p.cnt["s"]
            SF(("stats_s", sb), "s", r_s)
            W("vector", "s", r_s)
            DO("vector", lambda e: e.tensor_scalar(
                stats[:, sb + 8:sb + 9], stats[:, sb + 1:sb + 2],
                stats[:, sb + 7:sb + 8], -1.0, ALU.mult, ALU.mult), inc="v")
            negmr_v = p.cnt["v"]
            return r_s, negmr_v

        def ln_apply(out_ap, src_ap, sb, r_s, negmr_v):
            W("scalar", "s", r_s)
            W("scalar", "v", negmr_v)
            DO("scalar", lambda e: e.activation(
                out_ap, src_ap, AF.Identity, scale=stats[:, sb + 7:sb + 8],
                bias=stats[:, sb + 8:sb + 9]), inc="s")
            return p.cnt["s"]

        with ExitStack() as sa:
            ae = sa.enter_context
            q_fm = ae(nc.sbuf_tensor("q_fm", [128, 8 * T], BF16))
            k_lat = ae(nc.sbuf_tensor("k_lat", [128, 2 * S], BF16))
            v_lat = ae(nc.sbuf_tensor("v_lat", [128, 2 * S], BF16))
            rwkbT = ae(nc.sbuf_tensor("rwkbT", [128, 2 * D], BF16))
            wkaTs = ae(nc.sbuf_tensor("wkaT_s", [128, 8 * LAT], BF16))
            wvaTs = ae(nc.sbuf_tensor("wvaT_s", [128, 8 * LAT], BF16))
            wvbTs = ae(nc.sbuf_tensor("wvbT_s", [128, 2 * D], BF16))
            qrow_all = ae(nc.sbuf_tensor("qrow_all", [16, T], BF16))
            ehs = ae(nc.sbuf_tensor("eh_s", [128, 128], BF16))

            q_fm_v = q_fm[:].rearrange("p (c t) -> p c t", c=8)
            k_lat_v = k_lat[:].rearrange("p (c t) -> p c t", c=2)
            v_lat_v = v_lat[:].rearrange("p (c t) -> p c t", c=2)
            rwkbT_v = rwkbT[:].rearrange("p (c t) -> p c t", c=2)
            wkaT_v = wkaTs[:].rearrange("p (c t) -> p c t", c=8)
            wvaT_v = wvaTs[:].rearrange("p (c t) -> p c t", c=8)
            wvbT_v = wvbTs[:].rearrange("p (c t) -> p c t", c=2)

            DO("gpsimd", lambda e: e.dma_start(
                wkaTs[:].rearrange("p (c t) -> p c t", c=8),
                wkaT_d.rearrange("(c p) t -> p c t", p=128)), inc="dg", n=16)
            DO("gpsimd", lambda e: e.dma_start(
                wvaTs[:].rearrange("p (c t) -> p c t", c=8),
                wvaT_d.rearrange("(c p) t -> p c t", p=128)), inc="dg", n=16)
            DO("gpsimd", lambda e: e.dma_start(
                wvbTs[:].rearrange("p (c t) -> p c t", c=2),
                wvbT_d.rearrange("(c p) t -> p c t", p=128)), inc="dg", n=16)
            DO("gpsimd", lambda e: e.dma_start(ehs[:], eh_d), inc="dg", n=16)
            attw_dg = p.cnt["dg"]

            with ExitStack() as s0:
                e0 = s0.enter_context
                xn1 = e0(nc.sbuf_tensor("xn1_fm", [128, 8 * S], BF16))
                xch = [e0(nc.sbuf_tensor(f"xch{i}", [128, D], F32))
                       for i in (0, 1, 2)]
                xnt = [e0(nc.sbuf_tensor(f"xnt{i}", [128, D], F32))
                       for i in (0, 1, 2)]
                wqch = [e0(nc.sbuf_tensor(f"wqch{i}", [128, D], BF16)) for i in (0, 1)]
                wkbs = e0(nc.sbuf_tensor("wkb_s", [128, 8 * LAT], BF16))
                rwqT = e0(nc.sbuf_tensor("rwqT", [128, 8 * D], BF16))
                xn1_v = xn1[:].rearrange("p (c t) -> p c t", c=8)
                rwqT_v = rwqT[:].rearrange("p (c t) -> p c t", c=8)
                wkb_v = wkbs[:].rearrange("p (c t) -> p c t", c=8)

                W("tensor", "ds", consts_ds)
                slots = [(ti, hh) for ti in range(4) for hh in range(2)]
                slot_i = [0]

                def next_slot():
                    ti, hh = slots[slot_i[0] % 8]
                    slot_i[0] += 1
                    return ps[ti][:, hh * 512:(hh + 1) * 512], ("sl", ti, hh)

                qsq1 = [s0.enter_context(
                    nc.sbuf_tensor(f"qsq1_{i}", [128, 512], BF16))
                    for i in (0, 1)]
                w3stg = [s0.enter_context(
                    nc.sbuf_tensor("w3stg0", [128, D], BF16))]

                def emit_w3_stage(ic):
                    i = 0
                    p.wfull("gpsimd", f"gw{i}")
                    DO("gpsimd", lambda e, ic=ic, i=i: e.dma_start(
                        w3stg[i][:], w3T_d[ic * 128:(ic + 1) * 128, :]),
                        inc=f"gw{i}", n=16)
                    p.wfull("gpsimd", f"gw{i}")
                    DO("gpsimd", lambda e, ic=ic, i=i: e.dma_start(
                        w3st_d[ic * 128:(ic + 1) * 128, :], w3stg[i][:]),
                        inc=f"gw{i}", n=16)

                def emit_ln_chunk(mc):
                    i = mc % 3
                    sb = i * 16
                    WF("sync", ("xchq", i))
                    DO("sync", lambda e, a=xch[i], m=mc: e.dma_start(
                        a[:], xf[m * 128:(m + 1) * 128, :]), inc=f"dl{i}", n=16)
                    ldv = p.cnt[f"dl{i}"]
                    r_s, negmr_v = ln_stats2(
                        xch[i][:], xnt[i][:], sb,
                        src_ready=(("vector", f"dl{i}", ldv),
                                   ("scalar", f"dl{i}", ldv)),
                        sq_key=("xnt", i))
                    ap_s = ln_apply(xnt[i][:], xch[i][:], sb, r_s, negmr_v)
                    SF(("xchq", i), "s", ap_s)
                    SF(("stats", sb), "s", ap_s)
                    W("tensor", "s", ap_s)
                    for g in range(2):
                        sl, key = next_slot()
                        WF("tensor", key)
                        for k in range(4):
                            DO("tensor", lambda e, sl=sl, b=xnt[i], g=g, k=k:
                               e.transpose(
                                   sl[:, k * 128:(k + 1) * 128],
                                   b[:, (g * 4 + k) * 128:(g * 4 + k + 1) * 128],
                                   ident[:]),
                               inc="t" if k == 3 else None)
                        if g == 1:
                            SF(("xnt", i), "t", p.cnt["t"])
                        W("vector", "t", p.cnt["t"])
                        DO("vector", lambda e, sl=sl, m=mc, g=g: e.tensor_copy(
                            xn1_v[:, g * 4:(g + 1) * 4, m * 128:(m + 1) * 128],
                            sl[:].rearrange("p (c t) -> p c t", c=4)), inc="v")
                        SF(key, "v", p.cnt["v"])

                def emit_rwq(c):
                    i = c % 2
                    WF("gpsimd", ("wqch", i))
                    DO("gpsimd", lambda e, a=wqch[i], c=c: e.dma_start(
                        a[:], wq_d[c * 128:(c + 1) * 128, :]), inc=f"gq{i}", n=16)
                    p.wfull("tensor", f"gq{i}")
                    for g in range(2):
                        sl, key = next_slot()
                        WF("tensor", key)
                        for mi in range(4):
                            DO("tensor", lambda e, sl=sl, b=wqch[i], g=g,
                               mi=mi, c=c: e.matmul(
                                   sl[:, mi * 128:(mi + 1) * 128],
                                   b[:, (g * 4 + mi) * 128:
                                     (g * 4 + mi + 1) * 128],
                                   rtc[:, c * 128:(c + 1) * 128],
                                   start=True, stop=True),
                               inc="t" if mi == 3 else None)
                        if g == 1:
                            SF(("wqch", i), "t", p.cnt["t"])
                        W("vector", "t", p.cnt["t"])
                        DO("vector", lambda e, sl=sl, c=c, g=g: e.tensor_copy(
                            rwqT_v[:, g * 4:(g + 1) * 4, c * 128:(c + 1) * 128],
                            sl[:].rearrange("p (m o) -> p m o", m=4)), inc="v")
                        SF(key, "v", p.cnt["v"])

                def emit_rwkb():
                    DO("gpsimd", lambda e: e.dma_start(
                        wkbs[:].rearrange("p (c t) -> p c t", c=8),
                        wkb_d.rearrange("(c p) t -> p c t", p=128)),
                        inc="dg", n=16)
                    p.wfull("tensor", "dg")
                    for lm in range(2):
                        for g in range(2):
                            sl, key = next_slot()
                            WF("tensor", key)
                            for cc in range(4):
                                c = g * 4 + cc
                                DO("tensor", lambda e, sl=sl, lm=lm, c=c,
                                   cc=cc: e.matmul(
                                       sl[:, cc * 128:(cc + 1) * 128],
                                       wkb_v[:, c, lm * 128:(lm + 1) * 128],
                                       rtc[:, c * 128:(c + 1) * 128],
                                       start=True, stop=True),
                                   inc="t" if cc == 3 else None)
                            W("vector", "t", p.cnt["t"])
                            DO("vector", lambda e, sl=sl, lm=lm, g=g:
                               e.tensor_copy(
                                   rwkbT_v[:, lm, g * 512:(g + 1) * 512],
                                   sl[:]), inc="v")
                            SF(key, "v", p.cnt["v"])

                xn1_ready = {}

                def proj(wview, dst_v, oc, nt, evac_eng):
                    sl, key = next_slot()
                    WF("tensor", key)
                    W("tensor", "v", xn1_ready[nt])
                    for ic in range(8):
                        DO("tensor", lambda e, sl=sl, w=wview, ic=ic, oc=oc,
                           nt=nt: e.matmul(
                               sl, w[:, ic, oc * 128:(oc + 1) * 128],
                               xn1_v[:, ic, nt * 512:(nt + 1) * 512],
                               start=(ic == 0), stop=(ic == 7)),
                           inc="t" if ic == 7 else None)
                    mm = p.cnt["t"]
                    W(evac_eng, "t", mm)
                    if evac_eng == "vector":
                        DO("vector", lambda e, sl=sl, dv=dst_v, oc=oc, nt=nt:
                           e.tensor_copy(dv[:, oc, nt * 512:(nt + 1) * 512], sl),
                           inc="v")
                        SF(key, "v", p.cnt["v"])
                    else:
                        DO("scalar", lambda e, sl=sl, dv=dst_v, oc=oc, nt=nt:
                           e.activation(dv[:, oc, nt * 512:(nt + 1) * 512], sl,
                                        AF.Identity), inc="s")
                        SF(key, "s", p.cnt["s"])

                def emit_qrow(nt):
                    sl, key = next_slot()
                    for c in range(8):
                        i = c % 2
                        WF("vector", ("qsq1", i))
                        DO("vector", lambda e, i=i, c=c, nt=nt: e.tensor_mul(
                            qsq1[i][:], q_fm_v[:, c, nt * 512:(nt + 1) * 512],
                            q_fm_v[:, c, nt * 512:(nt + 1) * 512]), inc="v")
                        sqv = p.cnt["v"]
                        W("tensor", "v", sqv)
                        if c == 0:
                            WF("tensor", key)
                        DO("tensor", lambda e, sl=sl, i=i, c=c: e.matmul(
                            sl[0:16, :], ehs[:, c * 16:(c + 1) * 16],
                            qsq1[i][:], start=(c == 0), stop=(c == 7)),
                           inc="t")
                        SF(("qsq1", i), "t", p.cnt["t"])
                    W("vector", "t", p.cnt["t"])
                    DO("vector", lambda e, sl=sl, nt=nt: e.tensor_copy(
                        qrow_all[:, nt * 512:(nt + 1) * 512], sl[0:16, :]),
                       inc="v")
                    SF(key, "v", p.cnt["v"])

                q_groups = [(oc, nt) for nt in range(4) for oc in range(8)]
                qnt_ready = {}
                qrow_done = set()
                for mc in range(32):
                    emit_ln_chunk(mc)
                    if mc >= 16:
                        emit_w3_stage(mc - 16)
                    if mc % 4 == 3:
                        xn1_ready[mc // 4] = p.cnt["v"]
                        if mc == 3:
                            p.wfull("tensor", "dg")
                    if 8 <= mc < 16:
                        emit_rwq(mc - 8)
                    if mc == 16:
                        emit_rwkb()
                    if mc % 4 == 3:
                        nt = mc // 4
                        for lc in range(2):
                            proj(wkaT_v, k_lat_v, lc, nt, "vector")
                        for lc in range(2):
                            proj(wvaT_v, v_lat_v, lc, nt, "vector")
                    if mc == 17:
                        rwq_ready_v = p.cnt["v"]
                    if mc >= 18 and q_groups:
                        W("tensor", "v", rwq_ready_v)
                        for _ in range(3):
                            if q_groups:
                                oc, nt = q_groups.pop(0)
                                proj(rwqT_v, q_fm_v, oc, nt, "scalar")
                                if oc == 7:
                                    qnt_ready[nt] = p.cnt["s"]
                    for nt in range(4):
                        if nt in qnt_ready and nt not in qrow_done:
                            qrow_done.add(nt)
                            W("vector", "s", qnt_ready[nt])
                            emit_qrow(nt)
                while q_groups:
                    oc, nt = q_groups.pop(0)
                    proj(rwqT_v, q_fm_v, oc, nt, "scalar")
                    if oc == 7:
                        qnt_ready[nt] = p.cnt["s"]
                q_ready_s = p.cnt["s"]
                kv_ready_v = p.cnt["v"]
                for nt in range(4):
                    if nt not in qrow_done:
                        qrow_done.add(nt)
                        W("vector", "s", qnt_ready[nt])
                        emit_qrow(nt)
                qrow_all_v = p.cnt["v"]


                p.barrier()
                p.emit_block()

            with ExitStack() as sb:
                be = sb.enter_context
                attnout = be(nc.sbuf_tensor("attnout", [128, 8 * T], BF16))
                woTs = be(nc.sbuf_tensor("woT_s", [128, 8 * D], BF16))
                woT_v = woTs[:].rearrange("p (c t) -> p c t", c=8)
                attnout_v = attnout[:].rearrange("p (c t) -> p c t", c=8)
                s2 = sb.enter_context(ExitStack())
                e2 = s2.enter_context
                k_aug = [e2(nc.sbuf_tensor(f"k_aug{i}", [128, S], BF16))
                         for i in (0, 1)]
                q_aug = [e2(nc.sbuf_tensor(f"q_aug{i}", [128, T], BF16))
                         for i in (0, 1)]
                v2h = [e2(nc.sbuf_tensor(f"v2h{i}", [128, 32 * 130], BF16))
                       for i in (0, 1)]
                pT = [e2(nc.sbuf_tensor(f"pT{i}", [128, 1024], BF16))
                      for i in (0, 1, 2)]
                sqscr = e2(nc.sbuf_tensor("sqscr", [128, 8 * 512], BF16))
                sqscr_v = sqscr[:].rearrange("p (c t) -> p c t", c=8)
                kmxs = e2(nc.sbuf_tensor("kmxs", [1, 16], F32))
                stage65 = e2(nc.sbuf_tensor("stage65", [1, T], BF16))
                recip = e2(nc.sbuf_tensor("recip", [128, 512], F32))
                recipbf = e2(nc.sbuf_tensor("recipbf", [128, 512], BF16))
                recipb = e2(nc.sbuf_tensor("recipb", [64, 512], F32))
                anorm = [e2(nc.sbuf_tensor(f"anorm{i}", [64, 512], BF16))
                         for i in (0, 1)]
                v2h_v = [b[:].rearrange("p (c t) -> p c t", c=32) for b in v2h]

                DO("gpsimd", lambda e: e.dma_start(
                    woTs[:].rearrange("p (c t) -> p c t", c=8),
                    woT_d.rearrange("(c p) t -> p c t", p=128)), inc="dg", n=16)
                wo_dg = p.cnt["dg"]

                for i in (0, 1):
                    DO("vector", lambda e, b=v2h_v[i]: e.memset(
                        b[:, :, 64:65], 1.0), inc="v")
                    DO("vector", lambda e, b=v2h_v[i]: e.memset(
                        b[:, :, 129:130], 1.0), inc="v")
                    DO("vector", lambda e, b=k_aug[i]: e.memset(
                        b[64:128, :], 0.0), inc="v")
                    W("vector", "v", p.cnt["v"])
                    DO("vector", lambda e, b=k_aug[i]: e.memset(
                        b[64:66, :], -1.0), inc="v")
                    DO("vector", lambda e, b=q_aug[i]: e.memset(
                        b[64:128, :], 0.0), inc="v")
                ones_init_v = p.cnt["v"]
                W("tensor", "v", ones_init_v)

                hstate = {}
                pending = []

                def stages_for(H):
                    hb = H % 2
                    hc, hp = H // 2, H % 2
                    st = []

                    def expand(nt):
                        sl_i = nt % 2
                        sl = ps[3][0:64, sl_i * 512:(sl_i + 1) * 512]
                        WF("tensor", ("pd", sl_i))
                        for lc in range(2):
                            DO("tensor", lambda e, sl=sl, lc=lc, nt=nt:
                               e.matmul(sl,
                                        rwkbT_v[:, lc, H * 64:(H + 1) * 64],
                                        k_lat_v[:, lc, nt * 512:(nt + 1) * 512],
                                        start=(lc == 0), stop=(lc == 1)),
                               inc="t" if lc == 1 else None)
                        mm = p.cnt["t"]
                        W("vector", "t", mm)
                        if nt == 0:
                            WF("vector", ("kaug", hb))
                        DO("vector", lambda e, sl=sl, nt=nt:
                           e.tensor_copy(k_aug[hb][0:64, nt * 512:(nt + 1) * 512],
                                         sl), inc="v")
                        SF(("pd", sl_i), "v", p.cnt["v"])
                        if nt == 0:
                            hstate.setdefault(H, {})
                        if nt == 7:
                            hstate[H]["kaug_v"] = p.cnt["v"]

                    def square(nt):
                        W("vector", "v", p.cnt["v"])
                        WF("vector", ("sqscr", nt))
                        DO("vector", lambda e, nt=nt: e.tensor_mul(
                            sqscr_v[0:64, nt, :],
                            k_aug[hb][0:64, nt * 512:(nt + 1) * 512],
                            k_aug[hb][0:64, nt * 512:(nt + 1) * 512]), inc="v")
                        hstate[H][("sq", nt)] = p.cnt["v"]

                    def sqmm(nt):
                        sl_i = nt % 2
                        sl = ps[3][0:1, sl_i * 512:(sl_i + 1) * 512]
                        W("tensor", "v", hstate[H][("sq", nt)])
                        WF("tensor", ("pd", sl_i))
                        DO("tensor", lambda e, sl=sl, nt=nt: e.matmul(
                            sl, onesc[0:64, :], sqscr_v[0:64, nt, :],
                            start=True, stop=True), inc="t")
                        SF(("sqscr", nt), "t", p.cnt["t"])
                        W("vector", "t", p.cnt["t"])
                        WF("vector", ("kmx", nt))
                        DO("vector", lambda e, sl=sl, nt=nt: e.tensor_reduce(
                            kmxs[0:1, nt:nt + 1], sl, AXX, ALU.max), inc="v")
                        SF(("pd", sl_i), "v", p.cnt["v"])
                        SF(("kmx", nt), "v", p.cnt["v"])

                    def final():
                        W("vector", "v", p.cnt["v"])
                        WF("vector", ("kmx", 8))
                        DO("vector", lambda e: e.tensor_reduce(
                            kmxs[0:1, 8:9], kmxs[0:1, 0:8], AXX, ALU.max),
                           inc="v")
                        W("vector", "v", p.cnt["v"])
                        DO("vector", lambda e: e.tensor_scalar(
                            kmxs[0:1, 10:11], kmxs[0:1, 8:9], 0.5, None,
                            ALU.mult), inc="v")
                        half_v = p.cnt["v"]
                        for nt in range(8):
                            SF(("kmx", nt), "v", half_v)
                        SF(("kmx", 8), "v", half_v)
                        W("vector", "v", half_v)
                        WF("vector", ("st65",))
                        DO("vector", lambda e: e.tensor_scalar(
                            stage65[0:1, :], qrow_all[0:1, :], 0.0,
                            kmxs[0:1, 10:11], ALU.mult, ALU.add), inc="v")
                        hstate[H]["st65_v"] = p.cnt["v"]
                        SF(("kmx", 8), "v", p.cnt["v"])

                    def dmas():
                        W("gpsimd", "s", q_ready_s)
                        W("gpsimd", "v", qrow_all_v)
                        W("gpsimd", "v", ones_init_v)
                        WF("gpsimd", ("qaug", hb))
                        qb = hp * 64
                        DO("gpsimd", lambda e, qb=qb: e.dma_start(
                            q_aug[hb][0:64, :], q_fm_v[qb:qb + 64, hc, :]),
                            inc=f"ga{hb}", n=16)
                        DO("gpsimd", lambda e: e.dma_start(
                            q_aug[hb][64:65, :], qrow_all[hc * 2 + hp:
                                                          hc * 2 + hp + 1, :]),
                            inc=f"ga{hb}", n=16)
                        W("gpsimd", "v", hstate[H]["st65_v"])
                        DO("gpsimd", lambda e: e.dma_start(
                            q_aug[hb][65:66, :], stage65[0:1, :]),
                            inc=f"ga{hb}", n=16)
                        SF(("st65",), f"ga{hb}", p.cnt[f"ga{hb}"])

                    def vgroup(g):
                        vb = (H // 2) % 2
                        sl_i = g % 2
                        sl = ps[3][:, sl_i * 512:(sl_i + 1) * 512]
                        WF("tensor", ("pd", sl_i))
                        if g == 0:
                            WF("tensor", ("v2h", vb))
                        for kc4 in range(4):
                            for lc in range(2):
                                DO("tensor", lambda e, sl=sl, kc4=kc4,
                                   lc=lc, g=g: e.matmul(
                                       sl[:, kc4 * 128:(kc4 + 1) * 128],
                                       v_lat_v[:, lc,
                                               (g * 4 + kc4) * 128:
                                               (g * 4 + kc4 + 1) * 128],
                                       wvbT_v[:, lc, H * 64:H * 64 + 128],
                                       start=(lc == 0), stop=(lc == 1)),
                                   inc="t" if (kc4 == 3 and lc == 1)
                                   else None)
                        mm = p.cnt["t"]
                        W("vector", "t", mm)
                        WF("vector", ("v2h", vb))
                        for hh in range(2):
                            DO("vector", lambda e, sl=sl, vb=vb, g=g, hh=hh:
                               e.tensor_copy(
                                   v2h_v[vb][:, g * 4:(g + 1) * 4,
                                             hh * 65:hh * 65 + 64],
                                   sl[:].rearrange(
                                       "p (c d) -> p c d", c=4)[:, :,
                                       hh * 64:(hh + 1) * 64]),
                               inc="v" if hh == 1 else None)
                        SF(("pd", sl_i), "v", p.cnt["v"])
                        if g == 7:
                            SF(("v2h_ready", vb), "v", p.cnt["v"])

                    st.append(lambda: (expand(0), expand(1)))
                    st.append(lambda: (expand(2), expand(3), square(0), square(1)))
                    st.append(lambda: (expand(4), expand(5), square(2), square(3)))
                    st.append(lambda: (expand(6), expand(7), square(4), square(5)))
                    st.append(lambda: (square(6), square(7), sqmm(0), sqmm(1)))
                    st.append(lambda: (sqmm(2), sqmm(3)))
                    st.append(lambda: (sqmm(4), sqmm(5)))
                    st.append(lambda: (sqmm(6), sqmm(7)))
                    st.append(final)
                    st.append(dmas)
                    if H % 2 == 0:
                        for g in range(8):
                            st.append(lambda g=g: vgroup(g))
                    return st

                exp_sg = {}

                def emit_av(g, need_wait=True):
                    h, j, b = g // 64, (g // 16) % 4, g % 16
                    hp = h % 2
                    vb = (h // 2) % 2
                    ji = (g // 16) % 2
                    av = ps[2][0:65, ji * 512:(ji + 1) * 512]
                    i = g % 3
                    if need_wait:
                        W("tensor", "s", exp_sg[g])
                    if b == 0:
                        WF("tensor", ("av", ji))
                        if ("v2h_ready", vb) in p.free:
                            WF("tensor", ("v2h_ready", vb))
                    for kk in range(2):
                        kc = 2 * b + kk
                        DO("tensor", lambda e, av=av, kc=kc, kk=kk,
                           i=i, vb=vb, hp=hp,
                           st=(b == 0 and kk == 0),
                           sp=(b == 15 and kk == 1): e.matmul(
                               av,
                               v2h_v[vb][:, kc, hp * 65:hp * 65 + 65],
                               pT[i][:, kk * 512:(kk + 1) * 512],
                               start=st, stop=sp),
                           inc="t" if kk == 1 else None)
                    SF(("pT", i), "t", p.cnt["t"])
                    if g % 64 == 63:
                        SF(("kaug", h % 2), "t", p.cnt["t"])
                        SF(("qaug", h % 2), "t", p.cnt["t"])
                        if h % 2 == 1:
                            SF(("v2h", (h // 2) % 2), "t", p.cnt["t"])

                def emit_epilogue(g):
                    h, j = g // 64, (g // 16) % 4
                    hc, hp = h // 2, h % 2
                    ji = (g // 16) % 2
                    W("vector", "t", p.cnt["t"])
                    WF("vector", ("recip",))
                    DO("vector", lambda e, ji=ji: e.reciprocal(
                        recip[64:65, :], ps[2][64:65, ji * 512:(ji + 1) * 512]),
                        inc="v")
                    W("vector", "v", p.cnt["v"])
                    WF("vector", ("recipbf",))
                    DO("vector", lambda e: e.tensor_copy(
                        recipbf[64:65, :], recip[64:65, :]), inc="v")
                    rc_v = p.cnt["v"]
                    SF(("recip",), "v", rc_v)

                    def epi1(rc_v=rc_v, ji=ji):
                        W("tensor", "v", rc_v)
                        WF("tensor", ("pd", ji))
                        DO("tensor", lambda e, ji=ji: e.matmul(
                            ps[3][0:64, ji * 512:(ji + 1) * 512],
                            ones128[64:65, :], recipbf[64:65, :],
                            start=True, stop=True), inc="t")
                        hstate[("bc", ji)] = p.cnt["t"]
                        SF(("recipbf",), "t", p.cnt["t"])

                    def epi2(h=h, j=j, ji=ji, hp=hp, hc=hc):
                        W("vector", "t", hstate[("bc", ji)])
                        WF("vector", ("recipb",))
                        DO("vector", lambda e, ji=ji: e.tensor_copy(
                            recipb[:], ps[3][0:64, ji * 512:(ji + 1) * 512]),
                            inc="v")
                        SF(("pd", ji), "v", p.cnt["v"])
                        W("vector", "v", p.cnt["v"])
                        ai = j % 2
                        WF("vector", ("anorm", ai))
                        DO("vector", lambda e, ai=ai, ji=ji: e.tensor_tensor(
                            anorm[ai][:], ps[2][0:64, ji * 512:(ji + 1) * 512],
                            recipb[:], ALU.mult), inc="v")
                        an_v = p.cnt["v"]
                        SF(("av", ji), "v", an_v)
                        SF(("recipb",), "v", an_v)
                        W("gpsimd", "v", an_v)
                        DO("gpsimd", lambda e, ai=ai, hp=hp, hc=hc, j=j:
                           e.dma_start(
                               attnout_v[hp * 64:hp * 64 + 64, hc,
                                         j * 512:(j + 1) * 512],
                               anorm[ai][:]), inc=f"gn{ai}", n=16)
                        SF(("anorm", ai), f"gn{ai}", p.cnt[f"gn{ai}"])

                    pos = min(2, len(pending))
                    pending[pos:pos] = [epi1, epi2]

                for fn in stages_for(0):
                    fn()
                for ic in range(16, 32):
                    pending.append(lambda ic=ic: emit_w3_stage(ic))
                G = NH * 64
                for g in range(G):
                    h, j, b = g // 64, (g // 16) % 4, g % 16
                    hb = h % 2
                    sc_i = g % 2
                    pt_i = g % 3
                    sc = ps[sc_i]
                    if g % 64 == 0 and h + 1 < NH:
                        pending.extend(stages_for(h + 1))
                    if g >= 2:
                        W("tensor", "s", exp_sg[g - 2])
                    if g % 64 == 0:
                        W("tensor", "v", hstate[h]["kaug_v"])
                        p.wfull("tensor", f"ga{hb}")
                    for kk in range(2):
                        kc = 2 * b + kk
                        DO("tensor", lambda e, sc=sc, kk=kk, kc=kc,
                           hb=hb, j=j: e.matmul(
                               sc[:, kk * 512:(kk + 1) * 512],
                               k_aug[hb][0:128, kc * 128:(kc + 1) * 128],
                               q_aug[hb][0:128, j * 512:(j + 1) * 512],
                               start=True, stop=True),
                           inc="t" if kk == 1 else None)
                    sc_t = p.cnt["t"]
                    W("scalar", "t", sc_t)
                    WF("scalar", ("pT", pt_i))
                    DO("scalar", lambda e, sc=sc, i=pt_i: e.activation(
                        pT[i][:], sc[:, 0:1024], AF.Exp, scale=0.125),
                        inc="s")
                    exp_sg[g] = p.cnt["s"]
                    SF(("sc", sc_i), "s", exp_sg[g])
                    if g >= 2:
                        emit_av(g - 2, need_wait=False)
                        if (g - 2) % 16 == 15:
                            emit_epilogue(g - 2)
                    if b % 2 == 1 and pending:
                        pending.pop(0)()
                emit_av(G - 2)
                emit_av(G - 1)
                emit_epilogue(G - 1)
                while pending:
                    pending.pop(0)()

                attn_done_dg = p.cnt["dg"]
                p.barrier()
                p.emit_block()
                s2.close()

                with ExitStack() as s3:
                    e3 = s3.enter_context
                    xoch = [e3(nc.sbuf_tensor(f"xoch{i}", [128, D], F32))
                            for i in (0, 1)]
                    t3 = [e3(nc.sbuf_tensor(f"t3_{i}", [128, D], F32))
                          for i in (0, 1)]
                    x1c = [e3(nc.sbuf_tensor(f"x1c{i}", [128, D], F32))
                           for i in (0, 1)]
                    p.wfull("tensor", "gn0")
                    p.wfull("tensor", "gn1")
                    p.wfull("tensor", "dg")
                    for mc in range(16):
                        i = mc % 2
                        pp = ps[mc % 4]
                        WF("sync", ("xoch", i))
                        DO("sync", lambda e, a=xoch[i], m=mc: e.dma_start(
                            a[:], xf[m * 128:(m + 1) * 128, :]), inc=f"dl{i}",
                            n=16)
                        ldxv = p.cnt[f"dl{i}"]
                        WF("tensor", ("p3", mc % 4))
                        for nt in range(2):
                            for hcc in range(8):
                                DO("tensor", lambda e, a=pp, nt=nt, hcc=hcc,
                                   m=mc: e.matmul(
                                       a[:, nt * 512:(nt + 1) * 512],
                                       attnout_v[:, hcc, m * 128:(m + 1) * 128],
                                       woT_v[:, hcc, nt * 512:(nt + 1) * 512],
                                       start=(hcc == 0), stop=(hcc == 7)),
                                   inc="t" if (hcc == 7 and nt == 1) else None)
                        mm = p.cnt["t"]
                        sb = i * 16
                        r_s, negmr_v = ln_stats2(
                            pp[:, 0:1024], t3[i][:], sb,
                            src_ready=(("vector", "t", mm), ("scalar", "t", mm)),
                            sq_key=("t3", i))
                        ap_s = ln_apply(t3[i][:], pp[:, 0:1024], sb, r_s,
                                        negmr_v)
                        SF(("p3", mc % 4), "s", ap_s)
                        SF(("stats", sb), "s", ap_s)
                        W("vector", "s", ap_s)
                        W("vector", f"dl{i}", ldxv)
                        WF("vector", ("x1c", i))
                        DO("vector", lambda e, i=i: e.tensor_add(
                            x1c[i][:], t3[i][:], xoch[i][:]), inc="v")
                        x1_v = p.cnt["v"]
                        SF(("t3", i), "v", x1_v)
                        SF(("xoch", i), "v", x1_v)
                        W("sync", "v", x1_v)
                        DO("sync", lambda e, i=i, m=mc: e.dma_start(
                            x1sp_d[m * 128:(m + 1) * 128, :], x1c[i][:]),
                            inc=f"dst{i}", n=16)
                        SF(("x1c", i), f"dst{i}", p.cnt[f"dst{i}"])
                    p.barrier()
                    p.emit_block()

        with ExitStack() as s4:
            e4 = s4.enter_context
            abuf = e4(nc.sbuf_tensor("abuf", [128, 32 * 1024], BF16))
            bbuf = e4(nc.sbuf_tensor("bbuf", [128, 32 * 1024], BF16))
            x1c4 = [e4(nc.sbuf_tensor(f"x1c4_{i}", [128, D], F32))
                    for i in (0, 1)]
            xnt4 = [e4(nc.sbuf_tensor(f"xnt4_{i}", [128, D], F32))
                    for i in (0, 1)]
            silu_sb = [e4(nc.sbuf_tensor(f"silu{i}", [128, 1024], F32))
                       for i in (0, 1)]
            w1t = [e4(nc.sbuf_tensor(f"w1t{i}", [128, 8 * 256], BF16))
                   for i in (0, 1)]
            w2t = [e4(nc.sbuf_tensor(f"w2t{i}", [128, 8 * 256], BF16))
                   for i in (0, 1)]
            h1t = [e4(nc.sbuf_tensor(f"h1t{i}", [128, 32 * 128], BF16))
                   for i in (0, 1)]
            tf3 = [e4(nc.sbuf_tensor(f"tf3_{i}", [128, D], F32))
                   for i in (0, 1)]
            a_v = abuf[:].rearrange("p (c t) -> p c t", c=32)
            b_v = bbuf[:].rearrange("p (c t) -> p c t", c=32)
            w1t_v = [w[:].rearrange("p (c t) -> p c t", c=8) for w in w1t]
            w2t_v = [w[:].rearrange("p (c t) -> p c t", c=8) for w in w2t]
            h1t_v = [w[:].rearrange("p (c t) -> p c t", c=32) for w in h1t]

            for tt in range(2):
                for mc in range(8):
                    i = mc % 2
                    pp = ps[mc % 4]
                    row0 = tt * 1024 + mc * 128
                    WF("sync", ("x1c4", i))
                    p.wfull("sync", "dst0")
                    p.wfull("sync", "dst1")
                    DO("sync", lambda e, a=x1c4[i], r=row0: e.dma_start(
                        a[:], x1sp_d[r:r + 128, :]), inc=f"dl{i}", n=16)
                    ldv = p.cnt[f"dl{i}"]
                    sb = i * 16
                    r_s, negmr_v = ln_stats2(
                        x1c4[i][:], xnt4[i][:], sb,
                        src_ready=(("vector", f"dl{i}", ldv),
                                   ("scalar", f"dl{i}", ldv)),
                        sq_key=("xnt4", i))
                    ap_s = ln_apply(xnt4[i][:], x1c4[i][:], sb, r_s, negmr_v)
                    SF(("x1c4", i), "s", ap_s)
                    SF(("stats", sb), "s", ap_s)
                    W("tensor", "s", ap_s)
                    WF("tensor", ("p4a", mc % 4))
                    for k in range(8):
                        DO("tensor", lambda e, a=pp, b=xnt4[i], k=k: e.transpose(
                            a[:, k * 128:(k + 1) * 128],
                            b[:, k * 128:(k + 1) * 128], ident[:]),
                           inc="t" if k == 7 else None)
                    SF(("xnt4", i), "t", p.cnt["t"])
                    W("vector", "t", p.cnt["t"])
                    WF("vector", ("bxn", mc))
                    DO("vector", lambda e, a=pp, m=mc: e.tensor_copy(
                        b_v[:, 0:8, m * 128:(m + 1) * 128],
                        a[:].rearrange("p (c t) -> p c t", c=8)), inc="v")
                    SF(("p4a", i), "v", p.cnt["v"])
                xn2_v_ready = p.cnt["v"]

                W("tensor", "v", xn2_v_ready)
                for ob in range(16):
                    i = ob % 2
                    WF("gpsimd", ("w1t", i))
                    WF("gpsimd", ("w2t", i))
                    DO("gpsimd", lambda e, a=w1t[i], ob=ob: e.dma_start(
                        a[:].rearrange("p (c t) -> p c t", c=8), w1b_d[ob]),
                        inc=f"gw{i}", n=16)
                    DO("gpsimd", lambda e, a=w2t[i], ob=ob: e.dma_start(
                        a[:].rearrange("p (c t) -> p c t", c=8), w2b_d[ob]),
                        inc=f"gw{i}", n=16)
                    for oc2 in range(2):
                        oc = ob * 2 + oc2
                        pi = oc % 2
                        pX, pY = ps[2 * pi], ps[2 * pi + 1]
                        p.wfull("tensor", f"gw{i}")
                        WF("tensor", ("pB", pi))
                        for nt in range(2):
                            for ic in range(8):
                                DO("tensor", lambda e, a=pX, nt=nt, ic=ic,
                                   i=i, oc2=oc2: e.matmul(
                                       a[:, nt * 512:(nt + 1) * 512],
                                       w1t_v[i][:, ic,
                                               oc2 * 128:(oc2 + 1) * 128],
                                       b_v[:, ic, nt * 512:(nt + 1) * 512],
                                       start=(ic == 0), stop=(ic == 7)),
                                   inc="t" if (ic == 7 and nt == 1) else None)
                        mmx = p.cnt["t"]
                        for nt in range(2):
                            for ic in range(8):
                                DO("tensor", lambda e, a=pY, nt=nt, ic=ic,
                                   i=i, oc2=oc2: e.matmul(
                                       a[:, nt * 512:(nt + 1) * 512],
                                       w2t_v[i][:, ic,
                                               oc2 * 128:(oc2 + 1) * 128],
                                       b_v[:, ic, nt * 512:(nt + 1) * 512],
                                       start=(ic == 0), stop=(ic == 7)),
                                   inc="t" if (ic == 7 and nt == 1) else None)
                        mmy = p.cnt["t"]
                        W("scalar", "t", mmx)
                        WF("scalar", ("silu", pi))
                        DO("scalar", lambda e, a=pX, pi=pi, oc=oc: e.activation(
                            silu_sb[pi][:], a[:, 0:1024], AF.Silu,
                            scale=s1s[:, oc:oc + 1]), inc="s")
                        sil_s = p.cnt["s"]
                        W("vector", "t", mmy)
                        W("vector", "s", sil_s)
                        WF("vector", ("a_ch", oc))
                        DO("vector", lambda e, a=pY, pi=pi, oc=oc:
                           e.scalar_tensor_tensor(
                               a_v[:, oc, :], a[:, 0:1024], s2s[:, oc:oc + 1],
                               silu_sb[pi][:], ALU.mult, ALU.mult), inc="v")
                        av_ = p.cnt["v"]
                        SF(("silu", pi), "v", av_)
                        SF(("pB", pi), "v", av_)
                        SF(("pBx", pi), "s", sil_s)
                    SF(("w1t", i), "t", p.cnt["t"])
                    SF(("w2t", i), "t", p.cnt["t"])
                a_ready_v = p.cnt["v"]

                for sub, src_v, dst_v, wdram, ss in (
                        ("C", a_v, b_v, h1b_d, sh1s), ("D", b_v, a_v, h2b_d, sh2s)):
                    if sub == "C":
                        W("tensor", "v", a_ready_v)
                    for oc in range(32):
                        i = oc % 2
                        WF("gpsimd", ("h1t", i))
                        DO("gpsimd", lambda e, a=h1t[i], oc=oc, w=wdram:
                                e.dma_start(a[:].rearrange(
                                    "p (c t) -> p c t", c=32), w[oc]),
                                inc=f"gh{i}", n=16)
                        pi = oc % 2
                        pp = ps[2 + pi]
                        p.wfull("tensor", f"gh{i}")
                        WF("tensor", ("pCD", pi))
                        for nt in range(2):
                            for ic in range(32):
                                DO("tensor", lambda e, a=pp, nt=nt, ic=ic, i=i,
                                   sv=src_v: e.matmul(
                                       a[:, nt * 512:(nt + 1) * 512],
                                       h1t_v[i][:, ic, :],
                                       sv[:, ic, nt * 512:(nt + 1) * 512],
                                       start=(ic == 0), stop=(ic == 31)),
                                   inc="t" if (ic == 31 and nt == 1) else None)
                        mm = p.cnt["t"]
                        SF(("h1t", i), "t", mm)
                        W("scalar", "t", mm)
                        WF("scalar", ("cd_ch", sub, oc))
                        DO("scalar", lambda e, a=pp, oc=oc, dv=dst_v, ss=ss:
                           e.activation(dv[:, oc, :], a[:, 0:1024], AF.Gelu,
                                        scale=ss[:, oc:oc + 1]), inc="s")
                        SF(("pCD", pi), "s", p.cnt["s"])
                        SF(("cd_in", sub, oc), "s", p.cnt["s"])
                    cd_s = p.cnt["s"]
                    if sub == "C":
                        W("tensor", "s", cd_s)
                c_ready_s = p.cnt["s"]

                W("sync", "s", c_ready_s)
                W("sync", "t", p.cnt["t"])
                for ic in range(32):
                    DO("sync", lambda e, ic=ic: e.dma_start(
                        b_v[:, ic, :], w3st_d[ic * 128:(ic + 1) * 128, :]),
                        inc="ds", n=16)
                p.wfull("tensor", "ds")
                W("tensor", "s", c_ready_s)
                for mc in range(8):
                    i = mc % 2
                    pp = ps[mc % 4]
                    row0 = tt * 1024 + mc * 128
                    WF("sync", ("x1c4", i))
                    DO("sync", lambda e, a=x1c4[i], r=row0: e.dma_start(
                        a[:], x1sp_d[r:r + 128, :]), inc=f"dl{i}", n=16)
                    ldv = p.cnt[f"dl{i}"]
                    WF("tensor", ("p4e", mc % 4))
                    for nt in range(2):
                        for ic in range(32):
                            DO("tensor", lambda e, a=pp, nt=nt, ic=ic, m=mc:
                               e.matmul(
                                   a[:, nt * 512:(nt + 1) * 512],
                                   a_v[:, ic, m * 128:(m + 1) * 128],
                                   b_v[:, ic, nt * 512:(nt + 1) * 512],
                                   start=(ic == 0), stop=(ic == 31)),
                               inc="t" if (ic == 31 and nt == 1) else None)
                    mm = p.cnt["t"]
                    W("vector", "t", mm)
                    WF("vector", ("tf3", i))
                    DO("vector", lambda e, a=pp, i=i: e.tensor_tensor(
                        tf3[i][:], a[:, 0:1024], s3bs[:], ALU.mult), inc="v")
                    f_v = p.cnt["v"]
                    SF(("p4e", mc % 4), "v", f_v)
                    sb = i * 16
                    r_s, negmr_v = ln_stats2(
                        tf3[i][:], xnt4[i][:], sb,
                        src_ready=(("vector", "v", f_v), ("scalar", "v", f_v)),
                        sq_key=("xnt4", i))
                    ap_s = ln_apply(xnt4[i][:], tf3[i][:], sb, r_s, negmr_v)
                    SF(("stats", sb), "s", ap_s)
                    W("vector", "s", ap_s)
                    W("vector", f"dl{i}", ldv)
                    DO("vector", lambda e, i=i: e.tensor_add(
                        tf3[i][:], xnt4[i][:], x1c4[i][:]), inc="v")
                    o_v = p.cnt["v"]
                    SF(("xnt4", i), "v", o_v)
                    SF(("x1c4", i), "v", o_v)
                    W("sync", "v", o_v)
                    DO("sync", lambda e, i=i, r=row0: e.dma_start(
                        out_d[r:r + 128, :], tf3[i][:]), inc=f"dst{i}", n=16)
                    SF(("tf3", i), f"dst{i}", p.cnt[f"dst{i}"])
                W("gpsimd", "t", p.cnt["t"])

            p.barrier()
            p.emit_block()

    return nc


def _host_prep(L0_attn_wq, L0_attn_wk_a, L0_attn_wk_b, L0_attn_wv_a,
               L0_attn_wv_b, L0_attn_wo, L0_ffn_w1_w, L0_ffn_w1_s,
               L0_ffn_w2_w, L0_ffn_w2_s, L0_ffn_h1_w, L0_ffn_h1_s,
               L0_ffn_h2_w, L0_ffn_h2_s, L0_ffn_w3_w, L0_ffn_w3_s):
    f = np.float32
    C = np.ascontiguousarray

    R = np.eye(D, dtype=np.float64)
    inv = 10000.0 ** (-(np.arange(0, 16, 2, dtype=np.float64)) / HD)
    for h in range(NH):
        for j in range(8):
            ang = h * inv[j]
            c_, s_ = np.cos(ang), np.sin(ang)
            b0 = h * HD
            R[b0 + 2 * j, b0 + 2 * j] = c_
            R[b0 + 2 * j, b0 + 2 * j + 1] = -s_
            R[b0 + 2 * j + 1, b0 + 2 * j] = s_
            R[b0 + 2 * j + 1, b0 + 2 * j + 1] = c_
    R = R.astype(f)
    rtc = np.zeros((128, 8 * 128), dtype=f)
    for c in range(8):
        rtc[:, c * 128:(c + 1) * 128] = R[c * 128:(c + 1) * 128,
                                          c * 128:(c + 1) * 128].T

    eh = np.zeros((128, 128), dtype=f)
    for c in range(8):
        for h_ in range(NH):
            if h_ // 2 == c:
                rows = slice((h_ % 2) * 64, (h_ % 2) * 64 + 64)
                eh[rows, c * 16 + h_] = 0.5

    w1T = L0_ffn_w1_w.T
    w2T = L0_ffn_w2_w.T
    h1T = L0_ffn_h1_w.T
    h2T = L0_ffn_h2_w.T
    ins = {
        "wq": C(L0_attn_wq.astype(f)),
        "wkb": C(L0_attn_wk_b.astype(f)),
        "wkaT": C(L0_attn_wk_a.T.astype(f)),
        "wvaT": C(L0_attn_wv_a.T.astype(f)),
        "wvbT": C(L0_attn_wv_b.T.astype(f)),
        "woT": C(L0_attn_wo.T.astype(f)),
        "w1b": C(w1T.reshape(8, 128, 16, 256).transpose(2, 1, 0, 3).astype(f)),
        "w2b": C(w2T.reshape(8, 128, 16, 256).transpose(2, 1, 0, 3).astype(f)),
        "h1b": C(h1T.reshape(32, 128, 32, 128).transpose(2, 1, 0, 3).astype(f)),
        "h2b": C(h2T.reshape(32, 128, 32, 128).transpose(2, 1, 0, 3).astype(f)),
        "w3T": C(L0_ffn_w3_w.T.astype(f)),
        "s1": C(L0_ffn_w1_s.reshape(32, 128).T.astype(f)),
        "s2": C(L0_ffn_w2_s.reshape(32, 128).T.astype(f)),
        "sh1": C(L0_ffn_h1_s.reshape(32, 128).T.astype(f)),
        "sh2": C(L0_ffn_h2_s.reshape(32, 128).T.astype(f)),
        "s3b": C(np.broadcast_to(L0_ffn_w3_s.astype(f), (128, D))),
        "ident": np.eye(128, dtype=f),
        "eh": eh,
        "rtc": rtc,
        "ones128": np.ones((128, 64), dtype=f),
        "onesc": np.ones((128, 1), dtype=f),
    }
    return ins


def kernel(L0_attn_x, L0_attn_wq, L0_attn_wk_a, L0_attn_wk_b, L0_attn_wv_a,
           L0_attn_wv_b, L0_attn_wo, L0_ffn_w1_w, L0_ffn_w1_s, L0_ffn_w2_w,
           L0_ffn_w2_s, L0_ffn_h1_w, L0_ffn_h1_s, L0_ffn_h2_w, L0_ffn_h2_s,
           L0_ffn_w3_w, L0_ffn_w3_s):
    x = np.asarray(L0_attn_x, dtype=np.float32)
    shared = _host_prep(
        np.asarray(L0_attn_wq), np.asarray(L0_attn_wk_a),
        np.asarray(L0_attn_wk_b), np.asarray(L0_attn_wv_a),
        np.asarray(L0_attn_wv_b), np.asarray(L0_attn_wo),
        np.asarray(L0_ffn_w1_w), np.asarray(L0_ffn_w1_s),
        np.asarray(L0_ffn_w2_w), np.asarray(L0_ffn_w2_s),
        np.asarray(L0_ffn_h1_w), np.asarray(L0_ffn_h1_s),
        np.asarray(L0_ffn_h2_w), np.asarray(L0_ffn_h2_s),
        np.asarray(L0_ffn_w3_w), np.asarray(L0_ffn_w3_s))

    nc = build_nc()
    in_maps = []
    for c in range(8):
        b, hf = c // 2, c % 2
        own = x[b, hf * T:(hf + 1) * T]
        oth = x[b, (1 - hf) * T:(2 - hf) * T]
        m = dict(shared)
        m["xf"] = np.ascontiguousarray(np.concatenate([own, oth], axis=0))
        in_maps.append(m)

    res = run_bass_kernel_spmd(nc, in_maps, core_ids=list(range(8)))
    out = np.empty((4, S, D), dtype=np.float32)
    for c in range(8):
        b, hf = c // 2, c % 2
        out[b, hf * T:(hf + 1) * T] = res.results[c]["out"]
    return out


# version 17
# speedup vs baseline: 1.0430x; 1.0062x over previous
import numpy as np
import concourse.bass as bass
import concourse.mybir as mybir
from concourse.bass_utils import run_bass_kernel_spmd
from contextlib import ExitStack

F32 = mybir.dt.float32
BF16 = mybir.dt.bfloat16
AF = mybir.ActivationFunctionType
ALU = mybir.AluOpType
AXX = mybir.AxisListType.X

D = 1024
S = 4096
T = 2048
LAT = 256
NH = 16
HD = 64
DH = 4096
EPS = 1e-5

ENGINES = ("tensor", "vector", "scalar", "sync", "gpsimd")


class Prog:
    def __init__(self, nc):
        self.nc = nc
        self.ops = {k: [] for k in ENGINES}
        self.sem = {}
        self.cnt = {}
        self.free = {}

    def add_sem(self, stack, name):
        self.sem[name] = stack.enter_context(self.nc.semaphore(name))
        self.cnt[name] = 0

    def do(self, eng, fn, inc=None, n=1):
        if inc is None:
            self.ops[eng].append(fn)
            return None
        self.cnt[inc] += n
        s = self.sem[inc]
        self.ops[eng].append(lambda e, f=fn, s=s, n=n: f(e).then_inc(s, n))
        return self.cnt[inc]

    def wait(self, eng, sem, val):
        assert val <= self.cnt[sem], (sem, val, self.cnt[sem])
        if val <= 0:
            return
        s = self.sem[sem]
        self.ops[eng].append(lambda e, s=s, v=val: e.wait_ge(s, v))

    def wfull(self, eng, sem):
        self.wait(eng, sem, self.cnt[sem])

    def wait_free(self, eng, key):
        if key in self.free:
            sname, v = self.free[key]
            self.wait(eng, sname, v)

    def set_free(self, key, sname, v):
        self.free[key] = (sname, v)

    def barrier(self):
        tgt = {k: self.cnt[k] for k in self.sem}
        for eng in ENGINES:
            for sname, v in tgt.items():
                self.wait(eng, sname, v)

    def emit_block(self):
        nc = self.nc
        ops = self.ops
        with nc.Block() as blk:
            @blk.tensor
            def _(e):
                for op in ops["tensor"]:
                    op(e)

            @blk.vector
            def _(e):
                for op in ops["vector"]:
                    op(e)

            @blk.scalar
            def _(e):
                for op in ops["scalar"]:
                    op(e)

            @blk.sync
            def _(e):
                for op in ops["sync"]:
                    op(e)

            @blk.gpsimd
            def _(e):
                for op in ops["gpsimd"]:
                    op(e)
        self.ops = {k: [] for k in ENGINES}


def build_nc():
    nc = bass.Bass("TRN2", target_bir_lowering=False)

    xf = nc.dram_tensor("xf", [S, D], F32, kind="ExternalInput").ap()
    wq_d = nc.dram_tensor("wq", [D, D], F32, kind="ExternalInput").ap()
    wkb_d = nc.dram_tensor("wkb", [D, LAT], F32, kind="ExternalInput").ap()
    wkaT_d = nc.dram_tensor("wkaT", [D, LAT], F32, kind="ExternalInput").ap()
    wvaT_d = nc.dram_tensor("wvaT", [D, LAT], F32, kind="ExternalInput").ap()
    wvbT_d = nc.dram_tensor("wvbT", [LAT, D], F32, kind="ExternalInput").ap()
    woT_d = nc.dram_tensor("woT", [D, D], F32, kind="ExternalInput").ap()
    w1b_d = nc.dram_tensor("w1b", [16, 128, 8, 256], F32, kind="ExternalInput").ap()
    w2b_d = nc.dram_tensor("w2b", [16, 128, 8, 256], F32, kind="ExternalInput").ap()
    h1b_d = nc.dram_tensor("h1b", [32, 128, 32, 128], F32, kind="ExternalInput").ap()
    h2b_d = nc.dram_tensor("h2b", [32, 128, 32, 128], F32, kind="ExternalInput").ap()
    w3T_d = nc.dram_tensor("w3T", [DH, D], F32, kind="ExternalInput").ap()
    s1_d = nc.dram_tensor("s1", [128, 32], F32, kind="ExternalInput").ap()
    s2_d = nc.dram_tensor("s2", [128, 32], F32, kind="ExternalInput").ap()
    sh1_d = nc.dram_tensor("sh1", [128, 32], F32, kind="ExternalInput").ap()
    sh2_d = nc.dram_tensor("sh2", [128, 32], F32, kind="ExternalInput").ap()
    s3b_d = nc.dram_tensor("s3b", [128, D], F32, kind="ExternalInput").ap()
    ident_d = nc.dram_tensor("ident", [128, 128], F32, kind="ExternalInput").ap()
    rtc_d = nc.dram_tensor("rtc", [128, 8 * 128], F32, kind="ExternalInput").ap()
    ones128_d = nc.dram_tensor("ones128", [128, 64], F32, kind="ExternalInput").ap()
    onesc_d = nc.dram_tensor("onesc", [128, 1], F32, kind="ExternalInput").ap()
    eh_d = nc.dram_tensor("eh", [128, 128], F32, kind="ExternalInput").ap()
    out_d = nc.dram_tensor("out", [T, D], F32, kind="ExternalOutput").ap()
    x1sp_d = nc.dram_tensor("x1sp", [T, D], F32, kind="Internal").ap()
    w3st_d = nc.dram_tensor("w3st", [DH, D], BF16, kind="Internal").ap()

    with ExitStack() as outer:
        oe = outer.enter_context
        p = Prog(nc)
        for sname in ("dg", "ds", "t", "v", "s", "dl0", "dl1", "dl2",
                      "dst0", "dst1",
                      "gq0", "gq1", "ga0", "ga1", "gn0", "gn1", "gw0", "gw1",
                      "gh0", "gh1"):
            p.add_sem(outer, sname)

        ident = oe(nc.sbuf_tensor("ident_s", [128, 128], F32))
        rtc = oe(nc.sbuf_tensor("rtc_s", [128, 8 * 128], BF16))
        ones128 = oe(nc.sbuf_tensor("ones128_s", [128, 64], BF16))
        onesc = oe(nc.sbuf_tensor("onesc_s", [128, 1], BF16))
        s1s = oe(nc.sbuf_tensor("s1_s", [128, 32], F32))
        s2s = oe(nc.sbuf_tensor("s2_s", [128, 32], F32))
        sh1s = oe(nc.sbuf_tensor("sh1_s", [128, 32], F32))
        sh2s = oe(nc.sbuf_tensor("sh2_s", [128, 32], F32))
        s3bs = oe(nc.sbuf_tensor("s3b_s", [128, D], F32))
        stats = oe(nc.sbuf_tensor("stats", [128, 64], F32))
        ps = [oe(nc.psum_tensor(f"ps{i}", [128, 1024], F32)) for i in range(4)]

        DO, W, WF, SF = p.do, p.wait, p.wait_free, p.set_free

        DO("sync", lambda e: e.dma_start(ident[:], ident_d), inc="ds", n=16)
        DO("gpsimd", lambda e: e.dma_start(rtc[:], rtc_d), inc="gq0", n=16)
        DO("gpsimd", lambda e: e.dma_start(ones128[:], ones128_d), inc="dg", n=16)
        DO("gpsimd", lambda e: e.dma_start(onesc[:], onesc_d), inc="dg", n=16)
        for dst, src in ((s1s, s1_d), (s2s, s2_d), (sh1s, sh1_d), (sh2s, sh2_d),
                         (s3bs, s3b_d)):
            DO("sync", lambda e, d=dst, s_=src: e.dma_start(d[:], s_), inc="ds", n=16)
        consts_ds = p.cnt["ds"]
        consts_dg = p.cnt["dg"]

        def ln_stats2(src_ap, sq_ap, sb, src_ready=(), sq_key=None, width=1024):
            for eng, sem, v in src_ready:
                W(eng, sem, v)
            WF("vector", ("stats", sb))
            DO("vector", lambda e: e.tensor_reduce(
                stats[:, sb:sb + 1], src_ap, AXX, ALU.add), inc="v")
            W("vector", "v", p.cnt["v"])
            DO("vector", lambda e: e.tensor_scalar(
                stats[:, sb + 1:sb + 2], stats[:, sb:sb + 1], 1.0 / width, None,
                ALU.mult), inc="v")
            m_v = p.cnt["v"]
            WF("scalar", ("stats_s", sb))
            if sq_key:
                WF("scalar", sq_key)
            DO("scalar", lambda e: e.activation(
                sq_ap, src_ap, AF.Square,
                accum_out=stats[:, sb + 2:sb + 3]), inc="s")
            sq_s = p.cnt["s"]
            if sq_key:
                SF(sq_key, "s", sq_s)
            W("vector", "s", sq_s)
            DO("vector", lambda e: e.tensor_scalar(
                stats[:, sb + 3:sb + 4], stats[:, sb + 2:sb + 3], 1.0 / width,
                EPS, ALU.mult, ALU.add), inc="v")
            W("vector", "v", p.cnt["v"])
            DO("vector", lambda e: e.tensor_mul(
                stats[:, sb + 4:sb + 5], stats[:, sb + 1:sb + 2],
                stats[:, sb + 1:sb + 2]), inc="v")
            W("vector", "v", p.cnt["v"])
            DO("vector", lambda e: e.scalar_tensor_tensor(
                stats[:, sb + 5:sb + 6], stats[:, sb + 4:sb + 5], -1.0,
                stats[:, sb + 3:sb + 4], ALU.mult, ALU.add), inc="v")
            W("vector", "v", p.cnt["v"])
            DO("vector", lambda e: e.reciprocal(
                stats[:, sb + 6:sb + 7], stats[:, sb + 5:sb + 6]), inc="v")
            rec_v = p.cnt["v"]
            W("scalar", "s", p.cnt["s"])
            W("scalar", "v", rec_v)
            DO("scalar", lambda e: e.activation(
                stats[:, sb + 7:sb + 8], stats[:, sb + 6:sb + 7], AF.Sqrt),
                inc="s")
            r_s = p.cnt["s"]
            SF(("stats_s", sb), "s", r_s)
            W("vector", "s", r_s)
            DO("vector", lambda e: e.tensor_scalar(
                stats[:, sb + 8:sb + 9], stats[:, sb + 1:sb + 2],
                stats[:, sb + 7:sb + 8], -1.0, ALU.mult, ALU.mult), inc="v")
            negmr_v = p.cnt["v"]
            return r_s, negmr_v

        def ln_apply(out_ap, src_ap, sb, r_s, negmr_v):
            W("scalar", "s", r_s)
            W("scalar", "v", negmr_v)
            DO("scalar", lambda e: e.activation(
                out_ap, src_ap, AF.Identity, scale=stats[:, sb + 7:sb + 8],
                bias=stats[:, sb + 8:sb + 9]), inc="s")
            return p.cnt["s"]

        with ExitStack() as sa:
            ae = sa.enter_context
            q_fm = ae(nc.sbuf_tensor("q_fm", [128, 8 * T], BF16))
            k_lat = ae(nc.sbuf_tensor("k_lat", [128, 2 * S], BF16))
            v_lat = ae(nc.sbuf_tensor("v_lat", [128, 2 * S], BF16))
            rwkbT = ae(nc.sbuf_tensor("rwkbT", [128, 2 * D], BF16))
            wkaTs = ae(nc.sbuf_tensor("wkaT_s", [128, 8 * LAT], BF16))
            wvaTs = ae(nc.sbuf_tensor("wvaT_s", [128, 8 * LAT], BF16))
            wvbTs = ae(nc.sbuf_tensor("wvbT_s", [128, 2 * D], BF16))
            qrow_all = ae(nc.sbuf_tensor("qrow_all", [16, T], BF16))
            ehs = ae(nc.sbuf_tensor("eh_s", [128, 128], BF16))

            q_fm_v = q_fm[:].rearrange("p (c t) -> p c t", c=8)
            k_lat_v = k_lat[:].rearrange("p (c t) -> p c t", c=2)
            v_lat_v = v_lat[:].rearrange("p (c t) -> p c t", c=2)
            rwkbT_v = rwkbT[:].rearrange("p (c t) -> p c t", c=2)
            wkaT_v = wkaTs[:].rearrange("p (c t) -> p c t", c=8)
            wvaT_v = wvaTs[:].rearrange("p (c t) -> p c t", c=8)
            wvbT_v = wvbTs[:].rearrange("p (c t) -> p c t", c=2)

            DO("gpsimd", lambda e: e.dma_start(
                wkaTs[:].rearrange("p (c t) -> p c t", c=8),
                wkaT_d.rearrange("(c p) t -> p c t", p=128)), inc="dg", n=16)
            DO("gpsimd", lambda e: e.dma_start(
                wvaTs[:].rearrange("p (c t) -> p c t", c=8),
                wvaT_d.rearrange("(c p) t -> p c t", p=128)), inc="dg", n=16)
            DO("gpsimd", lambda e: e.dma_start(
                wvbTs[:].rearrange("p (c t) -> p c t", c=2),
                wvbT_d.rearrange("(c p) t -> p c t", p=128)), inc="dg", n=16)
            DO("gpsimd", lambda e: e.dma_start(ehs[:], eh_d), inc="dg", n=16)
            attw_dg = p.cnt["dg"]

            with ExitStack() as s0:
                e0 = s0.enter_context
                xn1 = e0(nc.sbuf_tensor("xn1_fm", [128, 8 * S], BF16))
                xch = [e0(nc.sbuf_tensor(f"xch{i}", [128, D], F32))
                       for i in (0, 1, 2)]
                xnt = [e0(nc.sbuf_tensor(f"xnt{i}", [128, D], F32))
                       for i in (0, 1, 2)]
                wqch = [e0(nc.sbuf_tensor(f"wqch{i}", [128, D], BF16)) for i in (0, 1)]
                wkbs = e0(nc.sbuf_tensor("wkb_s", [128, 8 * LAT], BF16))
                rwqT = e0(nc.sbuf_tensor("rwqT", [128, 8 * D], BF16))
                xn1_v = xn1[:].rearrange("p (c t) -> p c t", c=8)
                rwqT_v = rwqT[:].rearrange("p (c t) -> p c t", c=8)
                wkb_v = wkbs[:].rearrange("p (c t) -> p c t", c=8)

                W("tensor", "ds", consts_ds)
                slots = [(ti, hh) for ti in range(4) for hh in range(2)]
                slot_i = [0]

                def next_slot():
                    ti, hh = slots[slot_i[0] % 8]
                    slot_i[0] += 1
                    return ps[ti][:, hh * 512:(hh + 1) * 512], ("sl", ti, hh)

                qsq1 = [s0.enter_context(
                    nc.sbuf_tensor(f"qsq1_{i}", [128, 512], BF16))
                    for i in (0, 1)]
                def emit_w3_stage(ic):
                    i = ic % 2
                    WF("gpsimd", ("wqch", i))
                    p.wfull("gpsimd", f"gw{i}")
                    DO("gpsimd", lambda e, ic=ic, i=i: e.dma_start(
                        wqch[i][:], w3T_d[ic * 128:(ic + 1) * 128, :]),
                        inc=f"gw{i}", n=16)
                    p.wfull("gpsimd", f"gw{i}")
                    DO("gpsimd", lambda e, ic=ic, i=i: e.dma_start(
                        w3st_d[ic * 128:(ic + 1) * 128, :], wqch[i][:]),
                        inc=f"gw{i}", n=16)

                def emit_ln_chunk(mc):
                    i = mc % 3
                    sb = i * 16
                    WF("sync", ("xchq", i))
                    DO("sync", lambda e, a=xch[i], m=mc: e.dma_start(
                        a[:], xf[m * 128:(m + 1) * 128, :]), inc=f"dl{i}", n=16)
                    ldv = p.cnt[f"dl{i}"]
                    r_s, negmr_v = ln_stats2(
                        xch[i][:], xnt[i][:], sb,
                        src_ready=(("vector", f"dl{i}", ldv),
                                   ("scalar", f"dl{i}", ldv)),
                        sq_key=("xnt", i))
                    ap_s = ln_apply(xnt[i][:], xch[i][:], sb, r_s, negmr_v)
                    SF(("xchq", i), "s", ap_s)
                    SF(("stats", sb), "s", ap_s)
                    W("tensor", "s", ap_s)
                    for g in range(2):
                        sl, key = next_slot()
                        WF("tensor", key)
                        for k in range(4):
                            DO("tensor", lambda e, sl=sl, b=xnt[i], g=g, k=k:
                               e.transpose(
                                   sl[:, k * 128:(k + 1) * 128],
                                   b[:, (g * 4 + k) * 128:(g * 4 + k + 1) * 128],
                                   ident[:]),
                               inc="t" if k == 3 else None)
                        if g == 1:
                            SF(("xnt", i), "t", p.cnt["t"])
                        W("vector", "t", p.cnt["t"])
                        DO("vector", lambda e, sl=sl, m=mc, g=g: e.tensor_copy(
                            xn1_v[:, g * 4:(g + 1) * 4, m * 128:(m + 1) * 128],
                            sl[:].rearrange("p (c t) -> p c t", c=4)), inc="v")
                        SF(key, "v", p.cnt["v"])

                def emit_rwq(c):
                    i = c % 2
                    WF("gpsimd", ("wqch", i))
                    DO("gpsimd", lambda e, a=wqch[i], c=c: e.dma_start(
                        a[:], wq_d[c * 128:(c + 1) * 128, :]), inc=f"gq{i}", n=16)
                    p.wfull("tensor", "gq0")
                    p.wfull("tensor", f"gq{i}")
                    for g in range(2):
                        sl, key = next_slot()
                        WF("tensor", key)
                        for mi in range(4):
                            DO("tensor", lambda e, sl=sl, b=wqch[i], g=g,
                               mi=mi, c=c: e.matmul(
                                   sl[:, mi * 128:(mi + 1) * 128],
                                   b[:, (g * 4 + mi) * 128:
                                     (g * 4 + mi + 1) * 128],
                                   rtc[:, c * 128:(c + 1) * 128],
                                   start=True, stop=True),
                               inc="t" if mi == 3 else None)
                        if g == 1:
                            SF(("wqch", i), "t", p.cnt["t"])
                        W("vector", "t", p.cnt["t"])
                        DO("vector", lambda e, sl=sl, c=c, g=g: e.tensor_copy(
                            rwqT_v[:, g * 4:(g + 1) * 4, c * 128:(c + 1) * 128],
                            sl[:].rearrange("p (m o) -> p m o", m=4)), inc="v")
                        SF(key, "v", p.cnt["v"])

                def emit_rwkb():
                    DO("gpsimd", lambda e: e.dma_start(
                        wkbs[:].rearrange("p (c t) -> p c t", c=8),
                        wkb_d.rearrange("(c p) t -> p c t", p=128)),
                        inc="dg", n=16)
                    p.wfull("tensor", "dg")
                    for lm in range(2):
                        for g in range(2):
                            sl, key = next_slot()
                            WF("tensor", key)
                            for cc in range(4):
                                c = g * 4 + cc
                                DO("tensor", lambda e, sl=sl, lm=lm, c=c,
                                   cc=cc: e.matmul(
                                       sl[:, cc * 128:(cc + 1) * 128],
                                       wkb_v[:, c, lm * 128:(lm + 1) * 128],
                                       rtc[:, c * 128:(c + 1) * 128],
                                       start=True, stop=True),
                                   inc="t" if cc == 3 else None)
                            W("vector", "t", p.cnt["t"])
                            DO("vector", lambda e, sl=sl, lm=lm, g=g:
                               e.tensor_copy(
                                   rwkbT_v[:, lm, g * 512:(g + 1) * 512],
                                   sl[:]), inc="v")
                            SF(key, "v", p.cnt["v"])

                xn1_ready = {}

                def proj(wview, dst_v, oc, nt, evac_eng):
                    sl, key = next_slot()
                    WF("tensor", key)
                    W("tensor", "v", xn1_ready[nt])
                    for ic in range(8):
                        DO("tensor", lambda e, sl=sl, w=wview, ic=ic, oc=oc,
                           nt=nt: e.matmul(
                               sl, w[:, ic, oc * 128:(oc + 1) * 128],
                               xn1_v[:, ic, nt * 512:(nt + 1) * 512],
                               start=(ic == 0), stop=(ic == 7)),
                           inc="t" if ic == 7 else None)
                    mm = p.cnt["t"]
                    W(evac_eng, "t", mm)
                    if evac_eng == "vector":
                        DO("vector", lambda e, sl=sl, dv=dst_v, oc=oc, nt=nt:
                           e.tensor_copy(dv[:, oc, nt * 512:(nt + 1) * 512], sl),
                           inc="v")
                        SF(key, "v", p.cnt["v"])
                    else:
                        DO("scalar", lambda e, sl=sl, dv=dst_v, oc=oc, nt=nt:
                           e.activation(dv[:, oc, nt * 512:(nt + 1) * 512], sl,
                                        AF.Identity), inc="s")
                        SF(key, "s", p.cnt["s"])

                def emit_qrow(nt):
                    sl, key = next_slot()
                    for c in range(8):
                        i = c % 2
                        WF("vector", ("qsq1", i))
                        DO("vector", lambda e, i=i, c=c, nt=nt: e.tensor_mul(
                            qsq1[i][:], q_fm_v[:, c, nt * 512:(nt + 1) * 512],
                            q_fm_v[:, c, nt * 512:(nt + 1) * 512]), inc="v")
                        sqv = p.cnt["v"]
                        W("tensor", "v", sqv)
                        if c == 0:
                            WF("tensor", key)
                        DO("tensor", lambda e, sl=sl, i=i, c=c: e.matmul(
                            sl[0:16, :], ehs[:, c * 16:(c + 1) * 16],
                            qsq1[i][:], start=(c == 0), stop=(c == 7)),
                           inc="t")
                        SF(("qsq1", i), "t", p.cnt["t"])
                    W("vector", "t", p.cnt["t"])
                    DO("vector", lambda e, sl=sl, nt=nt: e.tensor_copy(
                        qrow_all[:, nt * 512:(nt + 1) * 512], sl[0:16, :]),
                       inc="v")
                    SF(key, "v", p.cnt["v"])

                q_groups = [(oc, nt) for nt in range(4) for oc in range(8)]
                qnt_ready = {}
                qrow_done = set()
                stage_q = list(range(32))
                for mc in range(32):
                    emit_ln_chunk(mc)
                    if mc < 8:
                        emit_rwq(mc)
                    if mc == 8:
                        emit_rwkb()
                    if mc >= 9:
                        for _ in range(2):
                            if stage_q:
                                emit_w3_stage(stage_q.pop(0))
                    if mc % 4 == 3:
                        xn1_ready[mc // 4] = p.cnt["v"]
                        if mc == 3:
                            p.wfull("tensor", "dg")
                    if mc % 4 == 3:
                        nt = mc // 4
                        for lc in range(2):
                            proj(wkaT_v, k_lat_v, lc, nt, "vector")
                        for lc in range(2):
                            proj(wvaT_v, v_lat_v, lc, nt, "vector")
                    if mc == 17:
                        rwq_ready_v = p.cnt["v"]
                    if mc >= 18 and q_groups:
                        W("tensor", "v", rwq_ready_v)
                        for _ in range(3):
                            if q_groups:
                                oc, nt = q_groups.pop(0)
                                proj(rwqT_v, q_fm_v, oc, nt, "scalar")
                                if oc == 7:
                                    qnt_ready[nt] = p.cnt["s"]
                    for nt in range(4):
                        if nt in qnt_ready and nt not in qrow_done:
                            qrow_done.add(nt)
                            W("vector", "s", qnt_ready[nt])
                            emit_qrow(nt)
                while q_groups:
                    oc, nt = q_groups.pop(0)
                    proj(rwqT_v, q_fm_v, oc, nt, "scalar")
                    if oc == 7:
                        qnt_ready[nt] = p.cnt["s"]
                q_ready_s = p.cnt["s"]
                kv_ready_v = p.cnt["v"]
                for nt in range(4):
                    if nt not in qrow_done:
                        qrow_done.add(nt)
                        W("vector", "s", qnt_ready[nt])
                        emit_qrow(nt)
                qrow_all_v = p.cnt["v"]


                p.barrier()
                p.emit_block()

            with ExitStack() as sb:
                be = sb.enter_context
                attnout = be(nc.sbuf_tensor("attnout", [128, 8 * T], BF16))
                woTs = be(nc.sbuf_tensor("woT_s", [128, 8 * D], BF16))
                woT_v = woTs[:].rearrange("p (c t) -> p c t", c=8)
                attnout_v = attnout[:].rearrange("p (c t) -> p c t", c=8)
                s2 = sb.enter_context(ExitStack())
                e2 = s2.enter_context
                k_aug = [e2(nc.sbuf_tensor(f"k_aug{i}", [128, S], BF16))
                         for i in (0, 1)]
                q_aug = [e2(nc.sbuf_tensor(f"q_aug{i}", [128, T], BF16))
                         for i in (0, 1)]
                v2h = [e2(nc.sbuf_tensor(f"v2h{i}", [128, 32 * 130], BF16))
                       for i in (0, 1)]
                pT = [e2(nc.sbuf_tensor(f"pT{i}", [128, 1024], BF16))
                      for i in (0, 1, 2)]
                sqscr = e2(nc.sbuf_tensor("sqscr", [128, 8 * 512], BF16))
                sqscr_v = sqscr[:].rearrange("p (c t) -> p c t", c=8)
                kmxs = e2(nc.sbuf_tensor("kmxs", [1, 16], F32))
                stage65 = e2(nc.sbuf_tensor("stage65", [1, T], BF16))
                recip = e2(nc.sbuf_tensor("recip", [128, 512], F32))
                recipbf = e2(nc.sbuf_tensor("recipbf", [128, 512], BF16))
                recipb = e2(nc.sbuf_tensor("recipb", [64, 512], F32))
                anorm = [e2(nc.sbuf_tensor(f"anorm{i}", [64, 512], BF16))
                         for i in (0, 1)]
                v2h_v = [b[:].rearrange("p (c t) -> p c t", c=32) for b in v2h]

                DO("gpsimd", lambda e: e.dma_start(
                    woTs[:].rearrange("p (c t) -> p c t", c=8),
                    woT_d.rearrange("(c p) t -> p c t", p=128)), inc="dg", n=16)
                wo_dg = p.cnt["dg"]

                for i in (0, 1):
                    DO("vector", lambda e, b=v2h_v[i]: e.memset(
                        b[:, :, 64:65], 1.0), inc="v")
                    DO("vector", lambda e, b=v2h_v[i]: e.memset(
                        b[:, :, 129:130], 1.0), inc="v")
                    DO("vector", lambda e, b=k_aug[i]: e.memset(
                        b[64:128, :], 0.0), inc="v")
                    W("vector", "v", p.cnt["v"])
                    DO("vector", lambda e, b=k_aug[i]: e.memset(
                        b[64:66, :], -1.0), inc="v")
                    DO("vector", lambda e, b=q_aug[i]: e.memset(
                        b[64:128, :], 0.0), inc="v")
                ones_init_v = p.cnt["v"]
                W("tensor", "v", ones_init_v)

                hstate = {}
                pending = []

                def stages_for(H):
                    hb = H % 2
                    hc, hp = H // 2, H % 2
                    st = []

                    def expand(nt):
                        sl_i = nt % 2
                        sl = ps[3][0:64, sl_i * 512:(sl_i + 1) * 512]
                        WF("tensor", ("pd", sl_i))
                        for lc in range(2):
                            DO("tensor", lambda e, sl=sl, lc=lc, nt=nt:
                               e.matmul(sl,
                                        rwkbT_v[:, lc, H * 64:(H + 1) * 64],
                                        k_lat_v[:, lc, nt * 512:(nt + 1) * 512],
                                        start=(lc == 0), stop=(lc == 1)),
                               inc="t" if lc == 1 else None)
                        mm = p.cnt["t"]
                        W("vector", "t", mm)
                        if nt == 0:
                            WF("vector", ("kaug", hb))
                        DO("vector", lambda e, sl=sl, nt=nt:
                           e.tensor_copy(k_aug[hb][0:64, nt * 512:(nt + 1) * 512],
                                         sl), inc="v")
                        SF(("pd", sl_i), "v", p.cnt["v"])
                        if nt == 0:
                            hstate.setdefault(H, {})
                        if nt == 7:
                            hstate[H]["kaug_v"] = p.cnt["v"]

                    def square(nt):
                        W("vector", "v", p.cnt["v"])
                        WF("vector", ("sqscr", nt))
                        DO("vector", lambda e, nt=nt: e.tensor_mul(
                            sqscr_v[0:64, nt, :],
                            k_aug[hb][0:64, nt * 512:(nt + 1) * 512],
                            k_aug[hb][0:64, nt * 512:(nt + 1) * 512]), inc="v")
                        hstate[H][("sq", nt)] = p.cnt["v"]

                    def sqmm(nt):
                        sl_i = nt % 2
                        sl = ps[3][0:1, sl_i * 512:(sl_i + 1) * 512]
                        W("tensor", "v", hstate[H][("sq", nt)])
                        WF("tensor", ("pd", sl_i))
                        DO("tensor", lambda e, sl=sl, nt=nt: e.matmul(
                            sl, onesc[0:64, :], sqscr_v[0:64, nt, :],
                            start=True, stop=True), inc="t")
                        SF(("sqscr", nt), "t", p.cnt["t"])
                        W("vector", "t", p.cnt["t"])
                        WF("vector", ("kmx", nt))
                        DO("vector", lambda e, sl=sl, nt=nt: e.tensor_reduce(
                            kmxs[0:1, nt:nt + 1], sl, AXX, ALU.max), inc="v")
                        SF(("pd", sl_i), "v", p.cnt["v"])
                        SF(("kmx", nt), "v", p.cnt["v"])

                    def final():
                        W("vector", "v", p.cnt["v"])
                        WF("vector", ("kmx", 8))
                        DO("vector", lambda e: e.tensor_reduce(
                            kmxs[0:1, 8:9], kmxs[0:1, 0:8], AXX, ALU.max),
                           inc="v")
                        W("vector", "v", p.cnt["v"])
                        DO("vector", lambda e: e.tensor_scalar(
                            kmxs[0:1, 10:11], kmxs[0:1, 8:9], 0.5, None,
                            ALU.mult), inc="v")
                        half_v = p.cnt["v"]
                        for nt in range(8):
                            SF(("kmx", nt), "v", half_v)
                        SF(("kmx", 8), "v", half_v)
                        W("vector", "v", half_v)
                        WF("vector", ("st65",))
                        DO("vector", lambda e: e.tensor_scalar(
                            stage65[0:1, :], qrow_all[0:1, :], 0.0,
                            kmxs[0:1, 10:11], ALU.mult, ALU.add), inc="v")
                        hstate[H]["st65_v"] = p.cnt["v"]
                        SF(("kmx", 8), "v", p.cnt["v"])

                    def dmas():
                        W("gpsimd", "s", q_ready_s)
                        W("gpsimd", "v", qrow_all_v)
                        W("gpsimd", "v", ones_init_v)
                        WF("gpsimd", ("qaug", hb))
                        qb = hp * 64
                        DO("gpsimd", lambda e, qb=qb: e.dma_start(
                            q_aug[hb][0:64, :], q_fm_v[qb:qb + 64, hc, :]),
                            inc=f"ga{hb}", n=16)
                        DO("gpsimd", lambda e: e.dma_start(
                            q_aug[hb][64:65, :], qrow_all[hc * 2 + hp:
                                                          hc * 2 + hp + 1, :]),
                            inc=f"ga{hb}", n=16)
                        W("gpsimd", "v", hstate[H]["st65_v"])
                        DO("gpsimd", lambda e: e.dma_start(
                            q_aug[hb][65:66, :], stage65[0:1, :]),
                            inc=f"ga{hb}", n=16)
                        SF(("st65",), f"ga{hb}", p.cnt[f"ga{hb}"])

                    def vgroup(g):
                        vb = (H // 2) % 2
                        sl_i = g % 2
                        sl = ps[3][:, sl_i * 512:(sl_i + 1) * 512]
                        WF("tensor", ("pd", sl_i))
                        if g == 0:
                            WF("tensor", ("v2h", vb))
                        for kc4 in range(4):
                            for lc in range(2):
                                DO("tensor", lambda e, sl=sl, kc4=kc4,
                                   lc=lc, g=g: e.matmul(
                                       sl[:, kc4 * 128:(kc4 + 1) * 128],
                                       v_lat_v[:, lc,
                                               (g * 4 + kc4) * 128:
                                               (g * 4 + kc4 + 1) * 128],
                                       wvbT_v[:, lc, H * 64:H * 64 + 128],
                                       start=(lc == 0), stop=(lc == 1)),
                                   inc="t" if (kc4 == 3 and lc == 1)
                                   else None)
                        mm = p.cnt["t"]
                        W("vector", "t", mm)
                        WF("vector", ("v2h", vb))
                        for hh in range(2):
                            DO("vector", lambda e, sl=sl, vb=vb, g=g, hh=hh:
                               e.tensor_copy(
                                   v2h_v[vb][:, g * 4:(g + 1) * 4,
                                             hh * 65:hh * 65 + 64],
                                   sl[:].rearrange(
                                       "p (c d) -> p c d", c=4)[:, :,
                                       hh * 64:(hh + 1) * 64]),
                               inc="v" if hh == 1 else None)
                        SF(("pd", sl_i), "v", p.cnt["v"])
                        if g == 7:
                            SF(("v2h_ready", vb), "v", p.cnt["v"])

                    st.append(lambda: (expand(0), expand(1)))
                    st.append(lambda: (expand(2), expand(3), square(0), square(1)))
                    st.append(lambda: (expand(4), expand(5), square(2), square(3)))
                    st.append(lambda: (expand(6), expand(7), square(4), square(5)))
                    st.append(lambda: (square(6), square(7), sqmm(0), sqmm(1)))
                    st.append(lambda: (sqmm(2), sqmm(3)))
                    st.append(lambda: (sqmm(4), sqmm(5)))
                    st.append(lambda: (sqmm(6), sqmm(7)))
                    st.append(final)
                    st.append(dmas)
                    if H % 2 == 0:
                        for g in range(8):
                            st.append(lambda g=g: vgroup(g))
                    return st

                exp_sg = {}

                def emit_av(g, need_wait=True):
                    h, j, b = g // 64, (g // 16) % 4, g % 16
                    hp = h % 2
                    vb = (h // 2) % 2
                    ji = (g // 16) % 2
                    av = ps[2][0:65, ji * 512:(ji + 1) * 512]
                    i = g % 3
                    if need_wait:
                        W("tensor", "s", exp_sg[g])
                    if b == 0:
                        WF("tensor", ("av", ji))
                        if ("v2h_ready", vb) in p.free:
                            WF("tensor", ("v2h_ready", vb))
                    for kk in range(2):
                        kc = 2 * b + kk
                        DO("tensor", lambda e, av=av, kc=kc, kk=kk,
                           i=i, vb=vb, hp=hp,
                           st=(b == 0 and kk == 0),
                           sp=(b == 15 and kk == 1): e.matmul(
                               av,
                               v2h_v[vb][:, kc, hp * 65:hp * 65 + 65],
                               pT[i][:, kk * 512:(kk + 1) * 512],
                               start=st, stop=sp),
                           inc="t" if kk == 1 else None)
                    SF(("pT", i), "t", p.cnt["t"])
                    if g % 64 == 63:
                        SF(("kaug", h % 2), "t", p.cnt["t"])
                        SF(("qaug", h % 2), "t", p.cnt["t"])
                        if h % 2 == 1:
                            SF(("v2h", (h // 2) % 2), "t", p.cnt["t"])

                def emit_epilogue(g):
                    h, j = g // 64, (g // 16) % 4
                    hc, hp = h // 2, h % 2
                    ji = (g // 16) % 2
                    W("vector", "t", p.cnt["t"])
                    WF("vector", ("recip",))
                    DO("vector", lambda e, ji=ji: e.reciprocal(
                        recip[64:65, :], ps[2][64:65, ji * 512:(ji + 1) * 512]),
                        inc="v")
                    W("vector", "v", p.cnt["v"])
                    WF("vector", ("recipbf",))
                    DO("vector", lambda e: e.tensor_copy(
                        recipbf[64:65, :], recip[64:65, :]), inc="v")
                    rc_v = p.cnt["v"]
                    SF(("recip",), "v", rc_v)

                    def epi1(rc_v=rc_v, ji=ji):
                        W("tensor", "v", rc_v)
                        WF("tensor", ("pd", ji))
                        DO("tensor", lambda e, ji=ji: e.matmul(
                            ps[3][0:64, ji * 512:(ji + 1) * 512],
                            ones128[64:65, :], recipbf[64:65, :],
                            start=True, stop=True), inc="t")
                        hstate[("bc", ji)] = p.cnt["t"]
                        SF(("recipbf",), "t", p.cnt["t"])

                    def epi2(h=h, j=j, ji=ji, hp=hp, hc=hc):
                        W("vector", "t", hstate[("bc", ji)])
                        WF("vector", ("recipb",))
                        DO("vector", lambda e, ji=ji: e.tensor_copy(
                            recipb[:], ps[3][0:64, ji * 512:(ji + 1) * 512]),
                            inc="v")
                        SF(("pd", ji), "v", p.cnt["v"])
                        W("vector", "v", p.cnt["v"])
                        ai = j % 2
                        WF("vector", ("anorm", ai))
                        DO("vector", lambda e, ai=ai, ji=ji: e.tensor_tensor(
                            anorm[ai][:], ps[2][0:64, ji * 512:(ji + 1) * 512],
                            recipb[:], ALU.mult), inc="v")
                        an_v = p.cnt["v"]
                        SF(("av", ji), "v", an_v)
                        SF(("recipb",), "v", an_v)
                        W("gpsimd", "v", an_v)
                        DO("gpsimd", lambda e, ai=ai, hp=hp, hc=hc, j=j:
                           e.dma_start(
                               attnout_v[hp * 64:hp * 64 + 64, hc,
                                         j * 512:(j + 1) * 512],
                               anorm[ai][:]), inc=f"gn{ai}", n=16)
                        SF(("anorm", ai), f"gn{ai}", p.cnt[f"gn{ai}"])

                    pos = min(2, len(pending))
                    pending[pos:pos] = [epi1, epi2]

                for fn in stages_for(0):
                    fn()
                G = NH * 64
                for g in range(G):
                    h, j, b = g // 64, (g // 16) % 4, g % 16
                    hb = h % 2
                    sc_i = g % 2
                    pt_i = g % 3
                    sc = ps[sc_i]
                    if g % 64 == 0 and h + 1 < NH:
                        pending.extend(stages_for(h + 1))
                    if g >= 2:
                        W("tensor", "s", exp_sg[g - 2])
                    if g % 64 == 0:
                        W("tensor", "v", hstate[h]["kaug_v"])
                        p.wfull("tensor", f"ga{hb}")
                    for kk in range(2):
                        kc = 2 * b + kk
                        DO("tensor", lambda e, sc=sc, kk=kk, kc=kc,
                           hb=hb, j=j: e.matmul(
                               sc[:, kk * 512:(kk + 1) * 512],
                               k_aug[hb][0:128, kc * 128:(kc + 1) * 128],
                               q_aug[hb][0:128, j * 512:(j + 1) * 512],
                               start=True, stop=True),
                           inc="t" if kk == 1 else None)
                    sc_t = p.cnt["t"]
                    W("scalar", "t", sc_t)
                    WF("scalar", ("pT", pt_i))
                    DO("scalar", lambda e, sc=sc, i=pt_i: e.activation(
                        pT[i][:], sc[:, 0:1024], AF.Exp, scale=0.125),
                        inc="s")
                    exp_sg[g] = p.cnt["s"]
                    SF(("sc", sc_i), "s", exp_sg[g])
                    if g >= 2:
                        emit_av(g - 2, need_wait=False)
                        if (g - 2) % 16 == 15:
                            emit_epilogue(g - 2)
                    if b % 2 == 1 and pending:
                        pending.pop(0)()
                emit_av(G - 2)
                emit_av(G - 1)
                emit_epilogue(G - 1)
                while pending:
                    pending.pop(0)()

                attn_done_dg = p.cnt["dg"]
                p.barrier()
                p.emit_block()
                s2.close()

                with ExitStack() as s3:
                    e3 = s3.enter_context
                    xoch = [e3(nc.sbuf_tensor(f"xoch{i}", [128, D], F32))
                            for i in (0, 1)]
                    t3 = [e3(nc.sbuf_tensor(f"t3_{i}", [128, D], F32))
                          for i in (0, 1)]
                    x1c = [e3(nc.sbuf_tensor(f"x1c{i}", [128, D], F32))
                           for i in (0, 1)]
                    p.wfull("tensor", "gn0")
                    p.wfull("tensor", "gn1")
                    p.wfull("tensor", "dg")
                    for mc in range(16):
                        i = mc % 2
                        pp = ps[mc % 4]
                        WF("sync", ("xoch", i))
                        DO("sync", lambda e, a=xoch[i], m=mc: e.dma_start(
                            a[:], xf[m * 128:(m + 1) * 128, :]), inc=f"dl{i}",
                            n=16)
                        ldxv = p.cnt[f"dl{i}"]
                        WF("tensor", ("p3", mc % 4))
                        for nt in range(2):
                            for hcc in range(8):
                                DO("tensor", lambda e, a=pp, nt=nt, hcc=hcc,
                                   m=mc: e.matmul(
                                       a[:, nt * 512:(nt + 1) * 512],
                                       attnout_v[:, hcc, m * 128:(m + 1) * 128],
                                       woT_v[:, hcc, nt * 512:(nt + 1) * 512],
                                       start=(hcc == 0), stop=(hcc == 7)),
                                   inc="t" if (hcc == 7 and nt == 1) else None)
                        mm = p.cnt["t"]
                        sb = i * 16
                        r_s, negmr_v = ln_stats2(
                            pp[:, 0:1024], t3[i][:], sb,
                            src_ready=(("vector", "t", mm), ("scalar", "t", mm)),
                            sq_key=("t3", i))
                        ap_s = ln_apply(t3[i][:], pp[:, 0:1024], sb, r_s,
                                        negmr_v)
                        SF(("p3", mc % 4), "s", ap_s)
                        SF(("stats", sb), "s", ap_s)
                        W("vector", "s", ap_s)
                        W("vector", f"dl{i}", ldxv)
                        WF("vector", ("x1c", i))
                        DO("vector", lambda e, i=i: e.tensor_add(
                            x1c[i][:], t3[i][:], xoch[i][:]), inc="v")
                        x1_v = p.cnt["v"]
                        SF(("t3", i), "v", x1_v)
                        SF(("xoch", i), "v", x1_v)
                        W("sync", "v", x1_v)
                        DO("sync", lambda e, i=i, m=mc: e.dma_start(
                            x1sp_d[m * 128:(m + 1) * 128, :], x1c[i][:]),
                            inc=f"dst{i}", n=16)
                        SF(("x1c", i), f"dst{i}", p.cnt[f"dst{i}"])
                    p.barrier()
                    p.emit_block()

        with ExitStack() as s4:
            e4 = s4.enter_context
            abuf = e4(nc.sbuf_tensor("abuf", [128, 32 * 1024], BF16))
            bbuf = e4(nc.sbuf_tensor("bbuf", [128, 32 * 1024], BF16))
            x1c4 = [e4(nc.sbuf_tensor(f"x1c4_{i}", [128, D], F32))
                    for i in (0, 1)]
            xnt4 = [e4(nc.sbuf_tensor(f"xnt4_{i}", [128, D], F32))
                    for i in (0, 1)]
            silu_sb = [e4(nc.sbuf_tensor(f"silu{i}", [128, 1024], F32))
                       for i in (0, 1)]
            w1t = [e4(nc.sbuf_tensor(f"w1t{i}", [128, 8 * 256], BF16))
                   for i in (0, 1)]
            w2t = [e4(nc.sbuf_tensor(f"w2t{i}", [128, 8 * 256], BF16))
                   for i in (0, 1)]
            h1t = [e4(nc.sbuf_tensor(f"h1t{i}", [128, 32 * 128], BF16))
                   for i in (0, 1)]
            tf3 = [e4(nc.sbuf_tensor(f"tf3_{i}", [128, D], F32))
                   for i in (0, 1)]
            a_v = abuf[:].rearrange("p (c t) -> p c t", c=32)
            b_v = bbuf[:].rearrange("p (c t) -> p c t", c=32)
            w1t_v = [w[:].rearrange("p (c t) -> p c t", c=8) for w in w1t]
            w2t_v = [w[:].rearrange("p (c t) -> p c t", c=8) for w in w2t]
            h1t_v = [w[:].rearrange("p (c t) -> p c t", c=32) for w in h1t]

            for tt in range(2):
                for mc in range(8):
                    i = mc % 2
                    pp = ps[mc % 4]
                    row0 = tt * 1024 + mc * 128
                    WF("sync", ("x1c4", i))
                    p.wfull("sync", "dst0")
                    p.wfull("sync", "dst1")
                    DO("sync", lambda e, a=x1c4[i], r=row0: e.dma_start(
                        a[:], x1sp_d[r:r + 128, :]), inc=f"dl{i}", n=16)
                    ldv = p.cnt[f"dl{i}"]
                    sb = i * 16
                    r_s, negmr_v = ln_stats2(
                        x1c4[i][:], xnt4[i][:], sb,
                        src_ready=(("vector", f"dl{i}", ldv),
                                   ("scalar", f"dl{i}", ldv)),
                        sq_key=("xnt4", i))
                    ap_s = ln_apply(xnt4[i][:], x1c4[i][:], sb, r_s, negmr_v)
                    SF(("x1c4", i), "s", ap_s)
                    SF(("stats", sb), "s", ap_s)
                    W("tensor", "s", ap_s)
                    WF("tensor", ("p4a", mc % 4))
                    for k in range(8):
                        DO("tensor", lambda e, a=pp, b=xnt4[i], k=k: e.transpose(
                            a[:, k * 128:(k + 1) * 128],
                            b[:, k * 128:(k + 1) * 128], ident[:]),
                           inc="t" if k == 7 else None)
                    SF(("xnt4", i), "t", p.cnt["t"])
                    W("vector", "t", p.cnt["t"])
                    WF("vector", ("bxn", mc))
                    DO("vector", lambda e, a=pp, m=mc: e.tensor_copy(
                        b_v[:, 0:8, m * 128:(m + 1) * 128],
                        a[:].rearrange("p (c t) -> p c t", c=8)), inc="v")
                    SF(("p4a", i), "v", p.cnt["v"])
                xn2_v_ready = p.cnt["v"]

                W("tensor", "v", xn2_v_ready)
                for ob in range(16):
                    i = ob % 2
                    WF("gpsimd", ("w1t", i))
                    WF("gpsimd", ("w2t", i))
                    DO("gpsimd", lambda e, a=w1t[i], ob=ob: e.dma_start(
                        a[:].rearrange("p (c t) -> p c t", c=8), w1b_d[ob]),
                        inc=f"gw{i}", n=16)
                    DO("gpsimd", lambda e, a=w2t[i], ob=ob: e.dma_start(
                        a[:].rearrange("p (c t) -> p c t", c=8), w2b_d[ob]),
                        inc=f"gw{i}", n=16)
                    for oc2 in range(2):
                        oc = ob * 2 + oc2
                        pi = oc % 2
                        pX, pY = ps[2 * pi], ps[2 * pi + 1]
                        p.wfull("tensor", f"gw{i}")
                        WF("tensor", ("pB", pi))
                        for nt in range(2):
                            for ic in range(8):
                                DO("tensor", lambda e, a=pX, nt=nt, ic=ic,
                                   i=i, oc2=oc2: e.matmul(
                                       a[:, nt * 512:(nt + 1) * 512],
                                       w1t_v[i][:, ic,
                                               oc2 * 128:(oc2 + 1) * 128],
                                       b_v[:, ic, nt * 512:(nt + 1) * 512],
                                       start=(ic == 0), stop=(ic == 7)),
                                   inc="t" if (ic == 7 and nt == 1) else None)
                        mmx = p.cnt["t"]
                        for nt in range(2):
                            for ic in range(8):
                                DO("tensor", lambda e, a=pY, nt=nt, ic=ic,
                                   i=i, oc2=oc2: e.matmul(
                                       a[:, nt * 512:(nt + 1) * 512],
                                       w2t_v[i][:, ic,
                                               oc2 * 128:(oc2 + 1) * 128],
                                       b_v[:, ic, nt * 512:(nt + 1) * 512],
                                       start=(ic == 0), stop=(ic == 7)),
                                   inc="t" if (ic == 7 and nt == 1) else None)
                        mmy = p.cnt["t"]
                        W("scalar", "t", mmx)
                        WF("scalar", ("silu", pi))
                        DO("scalar", lambda e, a=pX, pi=pi, oc=oc: e.activation(
                            silu_sb[pi][:], a[:, 0:1024], AF.Silu,
                            scale=s1s[:, oc:oc + 1]), inc="s")
                        sil_s = p.cnt["s"]
                        W("vector", "t", mmy)
                        W("vector", "s", sil_s)
                        WF("vector", ("a_ch", oc))
                        DO("vector", lambda e, a=pY, pi=pi, oc=oc:
                           e.scalar_tensor_tensor(
                               a_v[:, oc, :], a[:, 0:1024], s2s[:, oc:oc + 1],
                               silu_sb[pi][:], ALU.mult, ALU.mult), inc="v")
                        av_ = p.cnt["v"]
                        SF(("silu", pi), "v", av_)
                        SF(("pB", pi), "v", av_)
                        SF(("pBx", pi), "s", sil_s)
                    SF(("w1t", i), "t", p.cnt["t"])
                    SF(("w2t", i), "t", p.cnt["t"])
                a_ready_v = p.cnt["v"]

                for sub, src_v, dst_v, wdram, ss in (
                        ("C", a_v, b_v, h1b_d, sh1s), ("D", b_v, a_v, h2b_d, sh2s)):
                    if sub == "C":
                        W("tensor", "v", a_ready_v)
                    for oc in range(32):
                        i = oc % 2
                        WF("gpsimd", ("h1t", i))
                        DO("gpsimd", lambda e, a=h1t[i], oc=oc, w=wdram:
                                e.dma_start(a[:].rearrange(
                                    "p (c t) -> p c t", c=32), w[oc]),
                                inc=f"gh{i}", n=16)
                        pi = oc % 2
                        pp = ps[2 + pi]
                        p.wfull("tensor", f"gh{i}")
                        WF("tensor", ("pCD", pi))
                        for nt in range(2):
                            for ic in range(32):
                                DO("tensor", lambda e, a=pp, nt=nt, ic=ic, i=i,
                                   sv=src_v: e.matmul(
                                       a[:, nt * 512:(nt + 1) * 512],
                                       h1t_v[i][:, ic, :],
                                       sv[:, ic, nt * 512:(nt + 1) * 512],
                                       start=(ic == 0), stop=(ic == 31)),
                                   inc="t" if (ic == 31 and nt == 1) else None)
                        mm = p.cnt["t"]
                        SF(("h1t", i), "t", mm)
                        W("scalar", "t", mm)
                        WF("scalar", ("cd_ch", sub, oc))
                        DO("scalar", lambda e, a=pp, oc=oc, dv=dst_v, ss=ss:
                           e.activation(dv[:, oc, :], a[:, 0:1024], AF.Gelu,
                                        scale=ss[:, oc:oc + 1]), inc="s")
                        SF(("pCD", pi), "s", p.cnt["s"])
                        SF(("cd_in", sub, oc), "s", p.cnt["s"])
                    cd_s = p.cnt["s"]
                    if sub == "C":
                        W("tensor", "s", cd_s)
                c_ready_s = p.cnt["s"]

                W("sync", "s", c_ready_s)
                W("sync", "t", p.cnt["t"])
                for ic in range(32):
                    DO("sync", lambda e, ic=ic: e.dma_start(
                        b_v[:, ic, :], w3st_d[ic * 128:(ic + 1) * 128, :]),
                        inc="ds", n=16)
                p.wfull("tensor", "ds")
                W("tensor", "s", c_ready_s)
                for mc in range(8):
                    i = mc % 2
                    pp = ps[mc % 4]
                    row0 = tt * 1024 + mc * 128
                    WF("sync", ("x1c4", i))
                    DO("sync", lambda e, a=x1c4[i], r=row0: e.dma_start(
                        a[:], x1sp_d[r:r + 128, :]), inc=f"dl{i}", n=16)
                    ldv = p.cnt[f"dl{i}"]
                    WF("tensor", ("p4e", mc % 4))
                    for nt in range(2):
                        for ic in range(32):
                            DO("tensor", lambda e, a=pp, nt=nt, ic=ic, m=mc:
                               e.matmul(
                                   a[:, nt * 512:(nt + 1) * 512],
                                   a_v[:, ic, m * 128:(m + 1) * 128],
                                   b_v[:, ic, nt * 512:(nt + 1) * 512],
                                   start=(ic == 0), stop=(ic == 31)),
                               inc="t" if (ic == 31 and nt == 1) else None)
                    mm = p.cnt["t"]
                    W("vector", "t", mm)
                    WF("vector", ("tf3", i))
                    DO("vector", lambda e, a=pp, i=i: e.tensor_tensor(
                        tf3[i][:], a[:, 0:1024], s3bs[:], ALU.mult), inc="v")
                    f_v = p.cnt["v"]
                    SF(("p4e", mc % 4), "v", f_v)
                    sb = i * 16
                    r_s, negmr_v = ln_stats2(
                        tf3[i][:], xnt4[i][:], sb,
                        src_ready=(("vector", "v", f_v), ("scalar", "v", f_v)),
                        sq_key=("xnt4", i))
                    ap_s = ln_apply(xnt4[i][:], tf3[i][:], sb, r_s, negmr_v)
                    SF(("stats", sb), "s", ap_s)
                    W("vector", "s", ap_s)
                    W("vector", f"dl{i}", ldv)
                    DO("vector", lambda e, i=i: e.tensor_add(
                        tf3[i][:], xnt4[i][:], x1c4[i][:]), inc="v")
                    o_v = p.cnt["v"]
                    SF(("xnt4", i), "v", o_v)
                    SF(("x1c4", i), "v", o_v)
                    W("sync", "v", o_v)
                    DO("sync", lambda e, i=i, r=row0: e.dma_start(
                        out_d[r:r + 128, :], tf3[i][:]), inc=f"dst{i}", n=16)
                    SF(("tf3", i), f"dst{i}", p.cnt[f"dst{i}"])
                W("gpsimd", "t", p.cnt["t"])

            p.barrier()
            p.emit_block()

    return nc


def _host_prep(L0_attn_wq, L0_attn_wk_a, L0_attn_wk_b, L0_attn_wv_a,
               L0_attn_wv_b, L0_attn_wo, L0_ffn_w1_w, L0_ffn_w1_s,
               L0_ffn_w2_w, L0_ffn_w2_s, L0_ffn_h1_w, L0_ffn_h1_s,
               L0_ffn_h2_w, L0_ffn_h2_s, L0_ffn_w3_w, L0_ffn_w3_s):
    f = np.float32
    C = np.ascontiguousarray

    R = np.eye(D, dtype=np.float64)
    inv = 10000.0 ** (-(np.arange(0, 16, 2, dtype=np.float64)) / HD)
    for h in range(NH):
        for j in range(8):
            ang = h * inv[j]
            c_, s_ = np.cos(ang), np.sin(ang)
            b0 = h * HD
            R[b0 + 2 * j, b0 + 2 * j] = c_
            R[b0 + 2 * j, b0 + 2 * j + 1] = -s_
            R[b0 + 2 * j + 1, b0 + 2 * j] = s_
            R[b0 + 2 * j + 1, b0 + 2 * j + 1] = c_
    R = R.astype(f)
    rtc = np.zeros((128, 8 * 128), dtype=f)
    for c in range(8):
        rtc[:, c * 128:(c + 1) * 128] = R[c * 128:(c + 1) * 128,
                                          c * 128:(c + 1) * 128].T

    eh = np.zeros((128, 128), dtype=f)
    for c in range(8):
        for h_ in range(NH):
            if h_ // 2 == c:
                rows = slice((h_ % 2) * 64, (h_ % 2) * 64 + 64)
                eh[rows, c * 16 + h_] = 0.5

    w1T = L0_ffn_w1_w.T
    w2T = L0_ffn_w2_w.T
    h1T = L0_ffn_h1_w.T
    h2T = L0_ffn_h2_w.T
    ins = {
        "wq": C(L0_attn_wq.astype(f)),
        "wkb": C(L0_attn_wk_b.astype(f)),
        "wkaT": C(L0_attn_wk_a.T.astype(f)),
        "wvaT": C(L0_attn_wv_a.T.astype(f)),
        "wvbT": C(L0_attn_wv_b.T.astype(f)),
        "woT": C(L0_attn_wo.T.astype(f)),
        "w1b": C(w1T.reshape(8, 128, 16, 256).transpose(2, 1, 0, 3).astype(f)),
        "w2b": C(w2T.reshape(8, 128, 16, 256).transpose(2, 1, 0, 3).astype(f)),
        "h1b": C(h1T.reshape(32, 128, 32, 128).transpose(2, 1, 0, 3).astype(f)),
        "h2b": C(h2T.reshape(32, 128, 32, 128).transpose(2, 1, 0, 3).astype(f)),
        "w3T": C(L0_ffn_w3_w.T.astype(f)),
        "s1": C(L0_ffn_w1_s.reshape(32, 128).T.astype(f)),
        "s2": C(L0_ffn_w2_s.reshape(32, 128).T.astype(f)),
        "sh1": C(L0_ffn_h1_s.reshape(32, 128).T.astype(f)),
        "sh2": C(L0_ffn_h2_s.reshape(32, 128).T.astype(f)),
        "s3b": C(np.broadcast_to(L0_ffn_w3_s.astype(f), (128, D))),
        "ident": np.eye(128, dtype=f),
        "eh": eh,
        "rtc": rtc,
        "ones128": np.ones((128, 64), dtype=f),
        "onesc": np.ones((128, 1), dtype=f),
    }
    return ins


def kernel(L0_attn_x, L0_attn_wq, L0_attn_wk_a, L0_attn_wk_b, L0_attn_wv_a,
           L0_attn_wv_b, L0_attn_wo, L0_ffn_w1_w, L0_ffn_w1_s, L0_ffn_w2_w,
           L0_ffn_w2_s, L0_ffn_h1_w, L0_ffn_h1_s, L0_ffn_h2_w, L0_ffn_h2_s,
           L0_ffn_w3_w, L0_ffn_w3_s):
    x = np.asarray(L0_attn_x, dtype=np.float32)
    shared = _host_prep(
        np.asarray(L0_attn_wq), np.asarray(L0_attn_wk_a),
        np.asarray(L0_attn_wk_b), np.asarray(L0_attn_wv_a),
        np.asarray(L0_attn_wv_b), np.asarray(L0_attn_wo),
        np.asarray(L0_ffn_w1_w), np.asarray(L0_ffn_w1_s),
        np.asarray(L0_ffn_w2_w), np.asarray(L0_ffn_w2_s),
        np.asarray(L0_ffn_h1_w), np.asarray(L0_ffn_h1_s),
        np.asarray(L0_ffn_h2_w), np.asarray(L0_ffn_h2_s),
        np.asarray(L0_ffn_w3_w), np.asarray(L0_ffn_w3_s))

    nc = build_nc()
    in_maps = []
    for c in range(8):
        b, hf = c // 2, c % 2
        own = x[b, hf * T:(hf + 1) * T]
        oth = x[b, (1 - hf) * T:(2 - hf) * T]
        m = dict(shared)
        m["xf"] = np.ascontiguousarray(np.concatenate([own, oth], axis=0))
        in_maps.append(m)

    res = run_bass_kernel_spmd(nc, in_maps, core_ids=list(range(8)))
    out = np.empty((4, S, D), dtype=np.float32)
    for c in range(8):
        b, hf = c // 2, c % 2
        out[b, hf * T:(hf + 1) * T] = res.results[c]["out"]
    return out
